# version 20
# baseline (speedup 1.0000x reference)
import numpy as np
import ml_dtypes
import concourse.bass as bass
import concourse.mybir as mybir
from concourse.bass_utils import run_bass_kernel_spmd

F32 = mybir.dt.float32
BF16 = mybir.dt.bfloat16
U8 = mybir.dt.uint8
I32 = mybir.dt.int32
AF = mybir.ActivationFunctionType
OP = mybir.AluOpType
AX = mybir.AxisListType

TB = 1024
NT = 8
D = 1024
CC = 8
MSHIFT = 20.0
C1 = float(np.float32(2.0 ** 20 + 1))
MAGIC = float(np.float32(1.5 * 2.0 ** 14))
EPS_LN = 1e-5
DEBUG = False
import os
KTRUNC = int(os.environ.get("KTRUNC", "0"))

ENGS = ("sync", "scalar", "vector", "tensor", "gpsimd")
SEM_NAMES = ("SD", "SDS", "SG0", "SG1", "SG2", "SG3", "ST", "SV", "SS", "SP", "SOUT", "SOUT1")
SEM_OWNER = {"SD": "sync", "SDS": "scalar", "SG0": "gpsimd", "SG1": "gpsimd", "SG2": "gpsimd", "SG3": "gpsimd",
             "ST": "tensor", "SV": "vector", "SS": "scalar", "SP": "gpsimd",
             "SOUT": "gpsimd",
             "SOUT1": "gpsimd"}


def _sz(dtype):
    return {F32: 4, BF16: 2, U8: 1, I32: 4}[dtype]


class Prog:
    def __init__(self):
        self.pre = {e: [] for e in ENGS}
        self.bodies = [{e: [] for e in ENGS} for _ in range(4)]
        self.post = {e: [] for e in ENGS}
        self.cur = self.pre
        self.ctr = {s: 0 for s in SEM_NAMES}
        self.waited = {e: {} for e in ENGS}
        self.bank_free = {}

    def snapshot(self):
        return (dict(self.ctr), {e: dict(w) for e, w in self.waited.items()},
                dict(self.bank_free))

    def restore(self, s):
        self.ctr = dict(s[0])
        self.waited = {e: dict(w) for e, w in s[1].items()}
        self.bank_free = dict(s[2])

    def emit(self, eng, fn):
        self.cur[eng].append(fn)

    def wait(self, eng, sem, val):
        if val is None or val <= 0:
            return
        if self.waited[eng].get(sem, -1) >= val:
            return
        self.waited[eng][sem] = val
        self.emit(eng, lambda e, s=sem, v=val: e.wait_ge(SEMS[s], v))

    def bump(self, sem, n):
        self.ctr[sem] += n
        return self.ctr[sem]


SEMS = {}
P = Prog()


def build_nc():
    global P, SEMS
    P = Prog()
    SEMS = {}
    nc = bass.Bass("TRN2", target_bir_lowering=False,
                   detect_race_conditions=False)

    vid = nc.dram_tensor("vid", [1, 1], I32, kind="ExternalInput")
    xs = nc.dram_tensor("xs", [8 * TB, D], F32, kind="ExternalInput")
    xq = nc.dram_tensor("xq", [2 * TB, D], F32, kind="ExternalInput")
    wqkv = nc.dram_tensor("wqkv", [3 * D, D], F32, kind="ExternalInput")
    wout = nc.dram_tensor("wout", [D, D], F32, kind="ExternalInput")
    w8u = nc.dram_tensor("w8u", [D, D], U8, kind="ExternalInput")
    wscale = nc.dram_tensor("wscale", [1, 1], F32, kind="ExternalInput")
    fbias = nc.dram_tensor("fbias", [1, D], F32, kind="ExternalInput")
    outbias = nc.dram_tensor("outbias", [1, D], F32, kind="ExternalInput")
    cvec = nc.dram_tensor("cvec", [1, D], F32, kind="ExternalInput")
    wmod = nc.dram_tensor("wmod", [D, 6 * D], F32, kind="ExternalInput")
    stripd = nc.dram_tensor("stripd", [128, 1920], BF16, kind="ExternalInput")
    out = nc.dram_tensor("out", [2 * TB, D], F32, kind="ExternalOutput")
    w8dec = nc.dram_tensor("w8dec", [D, D], BF16, kind="Internal")
    identd = nc.dram_tensor("identd", [128, 128], BF16, kind="ExternalInput")
    dbg = {}
    if DEBUG:
        for nm, shape, dt in [("dbg_lnfm", [128, 8192], BF16),
                              ("dbg_k", [128, 8192], BF16),
                              ("dbg_v", [128, 8192], BF16),
                              ("dbg_q", [128, 8192], BF16),
                              ("dbg_pt", [128, 8192], BF16),
                              ("dbg_l", [1, 2048], F32),
                              ("dbg_row", [1, 1024], BF16),
                              ("dbg_row2", [1, 1024], BF16),
                              ("dbg_rowf", [1, 1024], F32),
                              ("dbg_h2", [128, 1024], F32),
                              ("dbg_vq", [128, 1024], F32),
                              ("dbg_o", [128, 8192], BF16),
                              ("dbg_x2", [128, 8192], BF16),
                              ("dbg_q8", [128, 8192], BF16),
                              ("dbg_w8t", [128, 8192], BF16)]:
            dbg[nm] = nc.dram_tensor(nm, shape, dt, kind="ExternalOutput")

    base0 = int(nc.sbuf_base)
    off = [base0]
    lim = int(nc.sbuf_top)

    def at(name, shape, dtype, place=None):
        if place is None:
            o = (off[0] + 31) & ~31
            off[0] = o + int(np.prod(shape[1:])) * _sz(dtype)
        else:
            o = place
        return nc.alloc_sbuf_tensor_at(name, shape, dtype, offset=o), o

    wkvT, o_wkv = at("wkvT", [128, 16384], BF16)
    wxT, o_wx = at("wxT", [128, 8192], BF16)
    QA, o_qa = at("QA", [128, 8192], BF16)
    QB, o_qb = at("QB", [128, 8192], BF16)
    OA, o_oa = at("OA", [128, 8192], BF16)
    OB, _ = at("OB", [128, 8192], BF16)
    xin, _ = at("xin", [128, 2048], F32)
    lntm, _ = at("lntm", [128, 2048], BF16)
    lnfm, o_lf = at("lnfm", [128, 8192], BF16)
    kfm, o_kf = at("kfm", [128, 8192], BF16)
    vtm, o_vt = at("vtm", [128, 8192], BF16)
    pT, _ = at("pT", [128, 8192], BF16)
    strip, _ = at("strip", [128, 1920], BF16, place=o_wx)
    wmb0, _ = at("wmb0", [128, 4096], BF16, place=o_kf)
    wmb1, _ = at("wmb1", [128, 4096], BF16, place=o_kf + 8192)
    qs0, _ = at("qs0", [128, 1024], F32, place=o_kf)
    qs1, _ = at("qs1", [128, 1024], F32, place=o_kf + 4096)
    qs2, _ = at("qs2", [128, 1024], F32, place=o_kf + 8192)
    ost0, _ = at("ost0", [128, 1024], F32, place=o_wkv)
    ost1, _ = at("ost1", [128, 1024], F32, place=o_wkv + 4096)
    reps, _ = at("reps", [128, 4096], BF16, place=o_vt)
    rlrep, _ = at("rlrep", [128, 1024], F32, place=o_vt + 8192)
    foldrep, _ = at("foldrep", [128, 1024], BF16, place=o_vt + 12288)
    x2buf, _ = at("x2buf", [128, 8192], BF16, place=o_lf)
    x2bufB, _ = at("x2bufB", [128, 8192], BF16, place=o_oa)
    sqv, _ = at("sqv", [128, 2048], F32, place=o_vt)
    wmf0, _ = at("wmf0", [128, 2048], F32, place=o_vt)
    wmf1, _ = at("wmf1", [128, 2048], F32, place=o_vt + 8192)
    sqx, _ = at("sqx", [128, 2048], F32, place=o_wx + 4096)
    sqe, _ = at("sqe", [128, 2048], F32, place=o_wkv + 8192)
    qs0b, _ = at("qs0b", [128, 1024], F32, place=o_wkv + 16384)
    qs1b, _ = at("qs1b", [128, 1024], F32, place=o_wkv + 20480)
    qs2b, _ = at("qs2b", [128, 1024], F32, place=o_wkv + 24576)
    w8T = pT
    q8tm = QA
    q8fm = QB
    wstage = QB
    wstage2 = QA
    vid_s, _ = at("vid_s", [1, 1], I32)
    stats, _ = at("stats", [128, 16], F32)
    ppv, _ = at("ppv", [128, 49], F32)
    ppv_bf, _ = at("ppv_bf", [128, 8], BF16)
    s_pp, _ = at("s_pp", [128, 8], BF16)
    bv_bf, _ = at("bv_bf", [128, 8], BF16)
    ones128, _ = at("ones128", [128, 1], BF16)
    negm, _ = at("negm", [128, 1], F32)
    ones1r, _ = at("ones1r", [1, 128], BF16)
    dect = [at(f"dec{j}", [128, 256], BF16)[0] for j in range(4)]
    u8in, _ = at("u8in", [128, 256], U8)
    msku, o_msku = at("msku", [128, 512], U8)
    ones2d, _ = at("ones2d", [128, 128], BF16, place=o_msku + 256)
    rowmm, _ = at("rowmm", [1, 1024], BF16)
    rowmm2, _ = at("rowmm2", [1, 1024], BF16)
    rowfa, _ = at("rowfa", [1, 1024], F32)
    lsb, _ = at("lsb", [1, 2048], F32)
    ppv2, _ = at("ppv2", [128, 32], F32)
    ident, _ = at("ident", [128, 128], BF16)
    assert off[0] <= lim, f"SBUF over budget: {off[0]} > {lim}"

    cb1_rep = reps[:, 0:1024]
    sb_rep = reps[:, 1024:2048]
    gab_rep = reps[:, 2048:3072]
    gbb_rep = reps[:, 3072:4096]

    pb = [nc.alloc_psum_tensor(f"pb{j}", [128, 512], F32) for j in range(8)]

    for s in SEM_NAMES:
        SEMS[s] = nc.alloc_semaphore(s)



    def dma(q, dst, src, transpose=False):
        sem = "SD" if q == "sync" else "SDS"
        v = P.bump(sem, 16)
        P.emit(q, lambda e, d=dst, s=src, t=transpose, sm=sem:
               e.dma_start(d, s, transpose=t).then_inc(SEMS[sm], 16))
        return v

    def gdma(dst, src, sem="SG0"):
        prev = P.ctr[sem]
        if prev:
            P.wait("gpsimd", sem, prev)
        v = P.bump(sem, 16)
        P.emit("gpsimd", lambda e, d=dst, s=src, sm=sem:
               e.dma_start(d, s).then_inc(SEMS[sm], 16))
        return v

    def odma(dst, src, sem="SOUT"):
        prev = P.ctr[sem]
        if prev:
            P.wait("gpsimd", sem, prev)
        v = P.bump(sem, 16)
        P.emit("gpsimd", lambda e, d=dst, s=src, sm=sem:
               e.dma_start(d, s).then_inc(SEMS[sm], 16))
        return v, sem

    def vnop():
        P.emit("vector", lambda e: e.engine_nop())
        P.emit("vector", lambda e: e.engine_nop())
        P.emit("vector", lambda e: e.engine_nop())

    def op(eng, fn, sem=None):
        if sem is None:
            P.emit(eng, fn)
            return None
        v = P.bump(sem, 1)
        P.emit(eng, lambda e, f=fn, sm=sem: f(e).then_inc(SEMS[sm], 1))
        return v

    def mm(outap, lhsT, rhs, start, stop, mark=False):
        if mark:
            v = P.bump("ST", 1)
            P.emit("tensor", lambda e, o=outap, l=lhsT, r=rhs, st=start, sp=stop:
                   e.matmul(o, l, r, start=st, stop=sp).then_inc(SEMS["ST"], 1))
            return v
        P.emit("tensor", lambda e, o=outap, l=lhsT, r=rhs, st=start, sp=stop:
               e.matmul(o, l, r, start=st, stop=sp))
        return None

    def use_bank(i):
        bf = P.bank_free.get(i)
        if bf is not None:
            P.wait("tensor", bf[0], bf[1])

    def bank_read(i, sem, val):
        P.bank_free[i] = (sem, val)

    def ln_tile(xtile, slot, sq):
        c0 = (slot % 2) * 8
        s1 = stats[:, c0:c0 + 1]
        s2 = stats[:, c0 + 1:c0 + 2]
        rstd = stats[:, c0 + 2:c0 + 3]
        mean = stats[:, c0 + 4:c0 + 5]
        var = stats[:, c0 + 5:c0 + 6]
        dst = lntm[:, (slot % 2) * 1024:(slot % 2) * 1024 + 1024]
        sqs = sq[:, (slot % 2) * 1024:(slot % 2) * 1024 + 1024]
        op("vector", lambda e: e.reduce_sum(s1, xtile, AX.X))
        op("vector", lambda e: e.tensor_mul(sqs, xtile, xtile))
        op("vector", lambda e: e.reduce_sum(s2, sqs, AX.X))
        vnop()
        op("vector", lambda e: e.tensor_scalar(mean, s1, 1.0 / D, None, OP.mult))
        op("vector", lambda e: e.tensor_scalar(s2, s2, 1.0 / D, EPS_LN,
                                               OP.mult, OP.add))
        op("vector", lambda e: e.scalar_tensor_tensor(var, mean, -1.0, mean,
                                                      OP.mult, OP.mult))
        vnop()
        op("vector", lambda e: e.tensor_add(var, var, s2))
        vnop()
        sv2 = op("vector", lambda e: e.tensor_scalar(sqs, xtile, mean,
                                                     None, OP.subtract), "SV")
        P.wait("scalar", "SV", sv2)
        op("scalar", lambda e: e.activation(rstd, var, AF.Ln))
        op("scalar", lambda e: e.activation(rstd, rstd, AF.Exp, scale=-0.5))
        ss3 = op("scalar", lambda e: e.activation(dst, sqs, AF.Identity,
                                                  scale=rstd), "SS")
        return ("SS", ss3)

    def pe_transpose_tile(t_idx, dstbuf, bank, ev):
        so = (t_idx % 2) * 1024
        stv, esem, ev_v = pe_tr8(lntm, so, dstbuf, 1024, t_idx * 128, bank, ev)
        return {"st": stv, "ss": ev_v if esem == "SS" else 0,
                "sv": ev_v if esem == "SV" else 0}

    def pe_tr8(src, src_base, dst, dst_stride, dst_off, bank, ev="scalar"):
        use_bank(bank)
        pbf = pb[bank].ap().bitcast(BF16)
        for cc in range(7):
            P.emit("tensor", lambda e, pbf=pbf, cc=cc, src=src, sb=src_base:
                   e.transpose(pbf[:, cc * 128:(cc + 1) * 128],
                               src[:, sb + cc * 128: sb + cc * 128 + 128],
                               ident[:, :]))
        stv = P.bump("ST", 1)
        P.emit("tensor", lambda e, pbf=pbf, src=src, sb=src_base:
               e.transpose(pbf[:, 896:1024], src[:, sb + 896: sb + 1024],
                           ident[:, :]).then_inc(SEMS["ST"], 1))
        sem = {"scalar": "SS", "vector": "SV"}[ev]
        P.wait(ev, "ST", stv)
        dv = (dst[:, 0:8 * dst_stride]
              .rearrange("p (cc c) -> p cc c", c=dst_stride)
              [:, :, dst_off:dst_off + 128])
        sv = pbf[:, 0:1024].rearrange("p (cc c) -> p cc c", c=128)
        if ev == "scalar":
            ev_v = op(ev, lambda e, dv=dv, sv=sv:
                      e.activation(dv, sv, AF.Copy), sem)
        else:
            ev_v = op(ev, lambda e, dv=dv, sv=sv:
                      e.tensor_copy(dv, sv), sem)
        bank_read(bank, sem, ev_v)
        return stv, sem, ev_v

    def row_to_pp(rowap, dsttile, dstcol, bias=0.0):
        last = None
        for t in range(8):
            bi = 4 + (t % 2)
            use_bank(bi)
            stv = mm(pb[bi][0:128, 0:1], rowap[0:1, t * 128:(t + 1) * 128],
                     ones1r[0:1, 0:1], True, True, mark=True)
            P.wait("scalar", "ST", stv)
            last = op("scalar", lambda e, t=t, bi=bi:
                      e.activation(dsttile[:, dstcol + t:dstcol + t + 1],
                                   pb[bi][0:128, 0:1], AF.Copy,
                                   bias=bias), "SS")
            bank_read(bi, "SS", last)
        return last

    ppbf_last_st = [0]

    def pp_to_row(pptile, colbase, dstrow):
        P.wait("vector", "ST", ppbf_last_st[0])
        svc = op("vector", lambda e: e.tensor_copy(ppv_bf[:, :],
                                                   pptile[:, colbase:colbase + 8]), "SV")
        P.wait("tensor", "SV", svc)
        last = None
        for t in range(8):
            bi = 4 + (t % 2)
            use_bank(bi)
            stv = mm(pb[bi][0:1, 0:128], ppv_bf[:, t:t + 1], ident[:, 0:128],
                     True, True, mark=True)
            ppbf_last_st[0] = stv
            P.wait("scalar", "ST", stv)
            last = op("scalar", lambda e, t=t, bi=bi, dstrow=dstrow:
                      e.activation(dstrow[0:1, t * 128:(t + 1) * 128],
                                   pb[bi][0:1, 0:128], AF.Copy), "SS")
            bank_read(bi, "SS", last)
        return last

    def outer_rep(rowap, dst, bias=0.0):
        stv = None
        for qh in range(2):
            bi = 4 + qh
            use_bank(bi)
            stv = mm(pb[bi][:, 0:512], ones1r[0:1, 0:128],
                     rowap[0:1, qh * 512:(qh + 1) * 512], True, True, mark=True)
        P.wait("scalar", "ST", stv)
        last = None
        for qh in range(2):
            bi = 4 + qh
            last = op("scalar", lambda e, qh=qh, bi=bi, dst=dst:
                      e.activation(dst[:, qh * 512:(qh + 1) * 512],
                                   pb[bi][:, 0:512], AF.Copy, bias=bias), "SS")
            bank_read(bi, "SS", last)
        return last

    v_vid = dma("sync", vid_s[0:1, 0:1], vid[0:1, 0:1])
    v_id = dma("sync", ident[:, :], identd[:, :])
    v_c = gdma(rowmm2[0:1, :], cvec[0:1, :], "SG0")
    v_ws = gdma(ppv[0:1, 48:49], wscale[0:1, 0:1], "SG0")
    op("vector", lambda e: e.memset(ones128[:, :], 1.0))
    op("vector", lambda e: e.memset(ones2d[:, :], 1.0))
    op("vector", lambda e: e.memset(ones1r[0:1, :], 1.0))
    sv_ones = op("vector", lambda e: e.memset(negm[:, :], -MSHIFT), "SV")

    P.wait("scalar", "SG0", v_c)
    op("scalar", lambda e: e.activation(rowfa[0:1, :], rowmm2[0:1, :], AF.Exp,
                                        scale=-1.0))
    ss0 = op("scalar", lambda e: e.activation(rowfa[0:1, :], rowfa[0:1, :],
                                              AF.Copy, bias=1.0), "SS")
    P.wait("vector", "SS", ss0)
    op("vector", lambda e: e.reciprocal(rowfa[0:1, :], rowfa[0:1, :]))
    vnop()
    sv_s = op("vector", lambda e: e.tensor_mul(rowmm[0:1, :], rowmm2[0:1, :],
                                               rowfa[0:1, :]), "SV")
    P.wait("tensor", "SV", sv_s)
    P.wait("tensor", "SV", sv_ones)
    ss_spp = row_to_pp(rowmm[0:1, :], s_pp, 0)
    P.wait("tensor", "SS", ss_spp)
    P.wait("vector", "SG0", v_ws)
    P.wait("vector", "ST", P.ctr["ST"])
    op("vector", lambda e: e.tensor_copy(rowmm[0:1, 0:1], ppv[0:1, 48:49]))
    vnop()
    op("vector", lambda e: e.tensor_sub(rowfa[0:1, 0:1], ppv[0:1, 48:49],
                                        rowmm[0:1, 0:1]))
    vnop()
    sv_wlo = op("vector", lambda e: e.tensor_copy(rowmm2[0:1, 0:1],
                                                  rowfa[0:1, 0:1]), "SV")
    P.wait("tensor", "SV", sv_wlo)
    use_bank(4)
    mm(pb[4][:, 0:1], ones1r[0:1, 0:128], rowmm[0:1, 0:1], True, False)
    stv = mm(pb[4][:, 0:1], ones1r[0:1, 0:128], rowmm2[0:1, 0:1], False, True,
             mark=True)
    P.wait("scalar", "ST", stv)
    ss_ws = op("scalar", lambda e: e.activation(ppv[:, 48:49], pb[4][:, 0:1],
                                                AF.Copy), "SS")
    bank_read(4, "SS", ss_ws)

    vload_q = {0: gdma(wstage[:].rearrange("p (s c) -> p s c", c=1024),
                       wqkv[0:D, :].rearrange("(s p) c -> p s c", p=128),
                       "SG2"),
               1: gdma(wstage2[:].rearrange("p (s c) -> p s c", c=1024),
                       wqkv[D:2 * D, :].rearrange("(s p) c -> p s c", p=128),
                       "SG3")}

    mod_tgt = {0: (ppv, 0, 0.0), 1: (ppv, 8, 1.0), 2: (ppv2, 0, 0.0),
               3: (ppv2, 8, 0.0), 4: (ppv2, 16, 1.0), 5: (ppv2, 24, 0.0)}
    mod_st = {}
    cast_sv = {}
    for g in range(12):
        buf = wmb0 if g % 2 == 0 else wmb1
        if g >= 2:
            P.wait("vector", "ST", mod_st[g - 2])
        for h in range(2):
            q = ("sync", "scalar")[h]
            wmf = (wmf0, wmf1)[h]
            if g >= 1:
                P.wait(q, "SV", cast_sv[(g - 1, h)])
            vl = dma(q, wmf[:].rearrange("p (cc j) -> p cc j", j=256),
                     wmod[:, g * 512 + h * 256: g * 512 + (h + 1) * 256]
                     .rearrange("(cc p) j -> p cc j", p=128))
            P.wait("vector", "SD" if h == 0 else "SDS", vl)
            dstv = (buf[:].rearrange("p (cc j) -> p cc j", j=512)
                    [:, :, h * 256:(h + 1) * 256])
            cast_sv[(g, h)] = op("vector", lambda e, dstv=dstv, wmf=wmf:
                                 e.tensor_copy(dstv, wmf[:].rearrange(
                                     "p (cc j) -> p cc j", j=256)), "SV")
        P.wait("tensor", "SV", cast_sv[(g, 1)])
        bi = 4 + g % 2
        use_bank(bi)
        stv = None
        for cc in range(CC):
            stv = mm(pb[bi][0:1, 0:512], s_pp[:, cc:cc + 1],
                     buf[:, cc * 512:(cc + 1) * 512],
                     cc == 0, cc == CC - 1, mark=(cc == CC - 1))
        mod_st[g] = stv
        P.wait("scalar", "ST", stv)
        prm, half = g // 2, g % 2
        ss = op("scalar", lambda e, bi=bi, half=half:
                e.activation(rowmm[0:1, half * 512:(half + 1) * 512],
                             pb[bi][0:1, 0:512], AF.Copy), "SS")
        bank_read(bi, "SS", ss)
        if half == 1:
            if prm in (2, 5):
                P.wait("vector", "SS", ss)
                op("vector", lambda e: e.tensor_scalar(rowfa[0:1, :],
                                                       rowmm[0:1, :], 0.0, None, OP.is_ge))
                vnop()
                op("vector", lambda e: e.tensor_scalar(rowfa[0:1, :],
                                                       rowfa[0:1, :], 2.0, -1.0, OP.mult, OP.add))
                ssa = op("scalar", lambda e: e.activation(rowmm2[0:1, :],
                                                          rowmm[0:1, :], AF.Abs), "SS")
                P.wait("vector", "SS", ssa)
                op("vector", lambda e: e.tensor_scalar(rowmm2[0:1, :],
                                                       rowmm2[0:1, :], 0.25, None, OP.max))
                vnop()
                svc = op("vector", lambda e: e.tensor_mul(rowmm[0:1, :],
                                                          rowfa[0:1, :], rowmm2[0:1, :]), "SV")
                P.wait("tensor", "SV", svc)
            P.wait("tensor", "SS", ss)
            tile, colbase, bias = mod_tgt[prm]
            row_to_pp(rowmm[0:1, :], tile, colbase, bias=bias)

    P.wait("vector", "SS", P.ctr["SS"])
    sv_caq = op("vector", lambda e: e.tensor_scalar(ppv[:, 16:24], ppv[:, 8:16],
                                                    1.0 / 32.0, None, OP.mult), "SV")
    sv_sabf = op("vector", lambda e: e.tensor_copy(ppv_bf[:, :], ppv[:, 0:8]),
                 "SV")

    P.wait("tensor", "SD", v_id)
    wq_tr = {}
    for grp in range(3):
        if grp == 2:
            P.wait("gpsimd", "ST", wq_tr[0]["st"])
            vload_q[2] = gdma(wstage[:].rearrange("p (s c) -> p s c", c=1024),
                              wqkv[2 * D:3 * D, :].rearrange(
                                  "(s p) c -> p s c", p=128), "SG2")
        stg = wstage2 if grp == 1 else wstage
        sem = "SG3" if grp == 1 else "SG2"
        P.wait("tensor", sem, vload_q[grp])
        last = {"st": 0, "ss": 0, "sv": 0}
        for s in range(8):
            if grp == 0:
                dst, dstride, doff = wxT, 1024, s * 128
            else:
                dst, dstride, doff = wkvT, 2048, (grp - 1) * 1024 + s * 128
            ev = ("scalar", "vector")[s % 2]
            stv, esem, ev_v = pe_tr8(stg, s * 1024, dst, dstride, doff,
                                     (0, 1, 2, 3)[s % 4], ev)
            last["st"] = stv
            last[esem.lower()] = max(last[esem.lower()], ev_v)
        wq_tr[grp] = last

    P.wait("tensor", "SS", max(wq_tr[g]["ss"] for g in range(3)))
    P.wait("tensor", "SV", max(wq_tr[g]["sv"] for g in range(3)))
    P.wait("tensor", "SV", sv_sabf)

    def bias_matvec(wsel, dstcol, scale):
        for jt in range(8):
            bi = 4 + (jt % 2)
            use_bank(bi)
            stv = None
            for cc in range(CC):
                if wsel == "q":
                    lh = wxT[:, cc * 1024 + jt * 128: cc * 1024 + jt * 128 + 128]
                else:
                    lh = wkvT[:, cc * 2048 + 1024 + jt * 128: cc * 2048 + 1024 + jt * 128 + 128]
                stv = mm(pb[bi][0:128, 0:1], lh, ppv_bf[:, cc:cc + 1],
                         cc == 0, cc == CC - 1, mark=(cc == CC - 1))
            ppbf_last_st[0] = stv
            P.wait("scalar", "ST", stv)
            ssx = op("scalar", lambda e, jt=jt, bi=bi:
                     e.activation(ppv[:, dstcol + jt:dstcol + jt + 1],
                                  pb[bi][0:128, 0:1], AF.Copy, scale=scale), "SS")
            bank_read(bi, "SS", ssx)

    bias_matvec("q", 32, 1.0 / 32.0)
    bias_matvec("v", 24, 1.0)

    P.wait("scalar", "SV", sv_caq)
    ss_wscale = None
    for cc in range(CC):
        ss_wscale = op("scalar", lambda e, cc=cc:
                       e.mul(wxT[:, cc * 1024:(cc + 1) * 1024],
                             wxT[:, cc * 1024:(cc + 1) * 1024],
                             ppv[:, 16 + cc:17 + cc]), "SS")
        ss_wscale = op("scalar", lambda e, cc=cc:
                       e.mul(wkvT[:, cc * 2048:(cc + 1) * 2048],
                             wkvT[:, cc * 2048:(cc + 1) * 2048],
                             ppv[:, 8 + cc:9 + cc]), "SS")

    P.wait("sync", "ST", mod_st[11])
    P.wait("scalar", "ST", mod_st[11])
    tr_hist = {}
    qgen_tr = {"ss": 0, "sv": 0}
    qg_lnv = {}
    for blk in range(2):
        dstbuf = lnfm if blk == 0 else kfm
        for qtr in range(4):
            v_x = dma("sync",
                      xin[:].rearrange("p (s c) -> p s c", c=1024),
                      xq[blk * TB + qtr * 256: blk * TB + (qtr + 1) * 256, :]
                      .rearrange("(s p) c -> p s c", p=128))
            P.wait("vector", "SD", v_x)
            P.wait("scalar", "SD", v_x)
            for tt in range(2):
                t_idx = qtr * 2 + tt
                g_idx = blk * 8 + t_idx
                if g_idx >= 2:
                    pv = tr_hist.get(g_idx - 2)
                    if pv:
                        P.wait("scalar", "ST", pv["st"])
                    if g_idx - 2 in qg_lnv:
                        P.wait("vector", *qg_lnv[g_idx - 2])
                lnv = ln_tile(xin[:, tt * 1024:(tt + 1) * 1024], t_idx, sqv)
                qg_lnv[g_idx] = lnv
                P.wait("tensor", *lnv)
                vals = pe_transpose_tile(t_idx, dstbuf, (6, 7, 2, 3)[t_idx % 4],
                                         ("scalar", "vector")[t_idx % 2])
                tr_hist[g_idx] = vals
                qgen_tr["ss"] = max(qgen_tr["ss"], vals["ss"])
                qgen_tr["sv"] = max(qgen_tr["sv"], vals["sv"])
            P.wait("sync", "SS", P.ctr["SS"])
            P.wait("sync", "SV", P.ctr["SV"])
    P.wait("tensor", "SS", qgen_tr["ss"])
    P.wait("tensor", "SV", qgen_tr["sv"])
    P.wait("tensor", "SS", ss_wscale)
    qb4 = [6, 7, 2, 3]
    for ct in range(8):
        for j in range(4):
            use_bank(qb4[j])
        stv = None
        for cc in range(CC):
            for j, (blk, qh) in enumerate([(0, 0), (0, 1), (1, 0), (1, 1)]):
                src = lnfm if blk == 0 else kfm
                stv = mm(pb[qb4[j]][:, 0:512],
                         wxT[:, cc * 1024 + ct * 128: cc * 1024 + ct * 128 + 128],
                         src[:, cc * 1024 + qh * 512: cc * 1024 + qh * 512 + 512],
                         cc == 0, cc == CC - 1,
                         mark=(cc == CC - 1 and j == 3))
        P.wait("scalar", "ST", stv)
        for j, (blk, qh) in enumerate([(0, 0), (0, 1), (1, 0), (1, 1)]):
            dstq = QA if blk == 0 else QB
            ssq = op("scalar", lambda e, ct=ct, j=j, dstq=dstq, qh=qh:
                     e.add(dstq[:, ct * 1024 + qh * 512: ct * 1024 + qh * 512 + 512],
                           pb[qb4[j]][:, 0:512], ppv[:, 32 + ct:33 + ct]), "SS")
            bank_read(qb4[j], "SS", ssq)

    P.wait("sync", "ST", P.ctr["ST"])
    v_strip = dma("sync", strip[:, :], stripd[:, :])
    P.wait("vector", "SD", v_strip)

    snap_base = P.snapshot()
    var_final = []
    for VAR in range(4):
        P.restore(snap_base)
        P.cur = P.bodies[VAR]
        iA, iB = VAR, 7 - VAR
        E = 8 - VAR
        decode_jobs = [(nr, h) for nr in range(8) for h in range(4)]
        dec_i = 0
        dec_store = 0
        dec_u8 = 0
        pv_done_st = P.ctr["ST"]
        exp_done_ss = P.ctr["SS"]
        kv_ss_hist = {}

        def decode_halftile(nr, h):
            nonlocal dec_store, dec_u8
            t0, t1, t2, t3 = dect
            P.wait("sync", "SV", dec_u8)
            v = dma("sync", u8in[:, :],
                    w8u[nr * 128:(nr + 1) * 128, h * 256:(h + 1) * 256])
            P.wait("vector", "SD", v)
            P.wait("vector", "SG1", dec_store)
            op("vector", lambda e: e.tensor_copy(t0[:, :], u8in[:, :]))
            vnop()
            op("vector", lambda e: e.tensor_scalar(
                t2[:, :], t0[:, :], 128.0, None, OP.is_ge))
            vnop()
            op("vector", lambda e: e.scalar_tensor_tensor(
                t0[:, :], t2[:, :], -128.0, t0[:, :], OP.mult, OP.add))
            vnop()
            op("vector", lambda e: e.tensor_scalar(
                msku[:, 0:256], t0[:, :], 8.0, None, OP.is_ge))
            op("vector", lambda e: e.tensor_scalar(
                t2[:, :], t0[:, :], 64.0, None, OP.is_ge))
            vnop()
            op("vector", lambda e: e.scalar_tensor_tensor(
                t0[:, :], t2[:, :], -64.0, t0[:, :], OP.mult, OP.add))
            op("vector", lambda e: e.tensor_scalar(
                t1[:, :], t2[:, :], 255.0, 1.0, OP.mult, OP.add))
            vnop()
            op("vector", lambda e: e.tensor_scalar(
                t2[:, :], t0[:, :], 32.0, None, OP.is_ge))
            vnop()
            op("vector", lambda e: e.scalar_tensor_tensor(
                t0[:, :], t2[:, :], -32.0, t0[:, :], OP.mult, OP.add))
            op("vector", lambda e: e.tensor_scalar(
                t3[:, :], t2[:, :], 15.0, 1.0, OP.mult, OP.add))
            vnop()
            op("vector", lambda e: e.tensor_mul(t1[:, :], t1[:, :], t3[:, :]))
            op("vector", lambda e: e.tensor_scalar(
                t2[:, :], t0[:, :], 16.0, None, OP.is_ge))
            vnop()
            op("vector", lambda e: e.scalar_tensor_tensor(
                t0[:, :], t2[:, :], -16.0, t0[:, :], OP.mult, OP.add))
            op("vector", lambda e: e.tensor_scalar(
                t3[:, :], t2[:, :], 3.0, 1.0, OP.mult, OP.add))
            vnop()
            op("vector", lambda e: e.tensor_mul(t1[:, :], t1[:, :], t3[:, :]))
            op("vector", lambda e: e.tensor_scalar(
                t2[:, :], t0[:, :], 8.0, None, OP.is_ge))
            vnop()
            op("vector", lambda e: e.scalar_tensor_tensor(
                t0[:, :], t2[:, :], -8.0, t0[:, :], OP.mult, OP.add))
            op("vector", lambda e: e.tensor_scalar(
                t3[:, :], t2[:, :], 1.0, 1.0, OP.mult, OP.add))
            vnop()
            op("vector", lambda e: e.tensor_mul(t1[:, :], t1[:, :], t3[:, :]))
            op("vector", lambda e: e.tensor_scalar(
                t3[:, :], t0[:, :], 2.0 ** -10, 8.0 * 2.0 ** -10, OP.mult, OP.add))
            vnop()
            op("vector", lambda e: e.tensor_mul(t3[:, :], t3[:, :], t1[:, :]))
            op("vector", lambda e: e.tensor_scalar(
                t0[:, :], t0[:, :], 2.0 ** -9, None, OP.mult))
            vnop()
            op("vector", lambda e: e.select(t1[:, :], msku[:, 0:256],
                                            t3[:, :], t0[:, :]))
            op("vector", lambda e: e.tensor_scalar(
                t2[:, :], u8in[:, :], 128.0, None, OP.is_ge))
            vnop()
            dec_u8 = op("vector", lambda e: e.tensor_scalar(
                t2[:, :], t2[:, :], -2.0, 1.0, OP.mult, OP.add), "SV")
            vnop()
            sv = op("vector", lambda e: e.tensor_mul(t1[:, :], t1[:, :],
                                                     t2[:, :]), "SV")
            P.wait("gpsimd", "SV", sv)
            dec_store = gdma(w8dec[nr * 128:(nr + 1) * 128, h * 256:(h + 1) * 256],
                             t1[:, :], "SG1")

        E_eff = min(E, KTRUNC) if KTRUNC else E
        ln_sv = {k2: ("SV", P.ctr["SV"]) for k2 in range(-8, 0)}

        def prep_load(k, t_idx):
            P.wait("gpsimd", *ln_sv[k * 8 + t_idx - 2])
            return gdma(xin[:, (t_idx % 2) * 1024:(t_idx % 2) * 1024 + 1024],
                        xs[k * TB + t_idx * 128: k * TB + (t_idx + 1) * 128, :],
                        "SG0")

        def prep_ln(k, t_idx, v_x):
            P.wait("vector", "SG0", v_x)
            g_idx = k * 8 + t_idx + 100
            pv = tr_hist.get(g_idx - 2)
            if pv:
                P.wait("scalar", "ST", pv["st"])
            P.wait("vector", *ln_sv[k * 8 + t_idx - 2])
            lnv = ln_tile(xin[:, (t_idx % 2) * 1024:(t_idx % 2) * 1024 + 1024],
                          t_idx, sqx)
            ln_sv[k * 8 + t_idx] = lnv

        def prep_tr(k, t_idx, tr_vals):
            g_idx = k * 8 + t_idx + 100
            P.wait("tensor", *ln_sv[k * 8 + t_idx])
            vals = pe_transpose_tile(t_idx, lnfm, (2, 3, 6, 7)[t_idx % 4],
                                     ("scalar", "vector")[t_idx % 2])
            tr_hist[g_idx] = vals
            tr_vals["ss"] = max(tr_vals["ss"], vals["ss"])
            tr_vals["sv"] = max(tr_vals["sv"], vals["sv"])

        tr_of = {}
        if E_eff > 0:
            tr_of[0] = {"ss": 0, "sv": 0}
            P.wait("vector", "ST", pv_done_st)
            for tt in range(8):
                v_x = prep_load(0, tt)
                prep_ln(0, tt, v_x)
                prep_tr(0, tt, tr_of[0])
        for k in range(E_eff):
            alive = ([("A", QA, OA, 0)] if k <= iA else []) + [("B", QB, OB, 1)]
            tr_vals = tr_of[k]
            P.wait("tensor", "SS", tr_vals["ss"])
            P.wait("tensor", "SV", tr_vals["sv"])
            gb4 = [2, 3, 6, 7]
            gbi = 0
            for ct in range(8):
                b0, b1 = gb4[gbi % 4], gb4[(gbi + 1) % 4]
                gbi += 2
                use_bank(b0)
                use_bank(b1)
                stv = None
                for cc in range(CC):
                    for kh, bi in ((0, b0), (1, b1)):
                        stv = mm(pb[bi][:, 0:512],
                                 wkvT[:, cc * 2048 + ct * 128: cc * 2048 + ct * 128 + 128],
                                 lnfm[:, cc * 1024 + kh * 512: cc * 1024 + kh * 512 + 512],
                                 cc == 0, cc == CC - 1,
                                 mark=(cc == CC - 1 and kh == 1))
                P.wait("scalar", "ST", stv)
                for kh, bi in ((0, b0), (1, b1)):
                    ssk = op("scalar", lambda e, ct=ct, kh=kh, bi=bi:
                             e.activation(kfm[:, ct * 1024 + kh * 512: ct * 1024 + kh * 512 + 512],
                                          pb[bi][:, 0:512], AF.Copy), "SS")
                    bank_read(bi, "SS", ssk)
            for tt in range(8):
                b0, b1 = gb4[gbi % 4], gb4[(gbi + 1) % 4]
                gbi += 2
                use_bank(b0)
                use_bank(b1)
                stv = None
                for cc in range(CC):
                    for dh, bi in ((0, b0), (1, b1)):
                        stv = mm(pb[bi][:, 0:512],
                                 lnfm[:, cc * 1024 + tt * 128: cc * 1024 + tt * 128 + 128],
                                 wkvT[:, cc * 2048 + 1024 + dh * 512: cc * 2048 + 1024 + dh * 512 + 512],
                                 cc == 0, cc == CC - 1,
                                 mark=(cc == CC - 1 and dh == 1))
                P.wait("scalar", "ST", stv)
                for dh, bi in ((0, b0), (1, b1)):
                    ssv = op("scalar", lambda e, tt=tt, dh=dh, bi=bi:
                             e.activation(vtm[:, tt * 1024 + dh * 512: tt * 1024 + dh * 512 + 512],
                                          pb[bi][:, 0:512], AF.Copy), "SS")
                    bank_read(bi, "SS", ssv)
            kv_evac_ss = P.ctr["SS"]
            if k + 1 < E_eff:
                tr_of[k + 1] = {"ss": 0, "sv": 0}
            if DEBUG and VAR == 0 and k == 0:
                P.wait("gpsimd", "SS", kv_evac_ss)
                odma(dbg["dbg_k"][:, :], kfm[:, :], sem="SG1")
                odma(dbg["dbg_v"][:, :], vtm[:, :], sem="SG1")
                odma(dbg["dbg_lnfm"][:, :], lnfm[:, :], sem="SG1")

            P.wait("tensor", "SS", kv_evac_ss)
            first_pair = True
            vx_next = 0
            for (nm, Qb, Ob, li) in alive:
                qblk = iA if nm == "A" else iB
                diag = (k == qblk)
                P.wait("scalar", "ST", pv_done_st)
                for kt in range(8):
                    sbk = (0, 1) if kt % 2 == 0 else (4, 5)
                    use_bank(sbk[0])
                    use_bank(sbk[1])
                    stv = None
                    for cc in range(CC):
                        for qh in range(2):
                            stv = mm(pb[sbk[qh]][:, 0:512],
                                     kfm[:, cc * 1024 + kt * 128: cc * 1024 + kt * 128 + 128],
                                     Qb[:, cc * 1024 + qh * 512: cc * 1024 + qh * 512 + 512],
                                     cc == 0, cc == CC - 1,
                                     mark=(cc == CC - 1 and qh == 1))
                    P.wait("scalar", "ST", stv)
                    for qh in range(2):
                        exp_done_ss = op("scalar", lambda e, kt=kt, qh=qh, b=sbk[qh]:
                                         e.activation(pT[:, kt * 1024 + qh * 512: kt * 1024 + qh * 512 + 512],
                                                      pb[b][:, 0:512], AF.Exp,
                                                      bias=negm[:, 0:1]), "SS")
                        bank_read(sbk[qh], "SS", exp_done_ss)
                    if diag:
                        soff = 896 - kt * 128
                        P.wait("vector", "SS", exp_done_ss)
                        op("vector", lambda e, kt=kt, soff=soff:
                           e.tensor_mul(pT[:, kt * 1024:(kt + 1) * 1024],
                                        pT[:, kt * 1024:(kt + 1) * 1024],
                                        strip[:, soff:soff + 1024]))
                    if first_pair and k + 1 < E_eff:
                        vx_next = prep_load(k + 1, kt)
                        prep_ln(k + 1, kt, vx_next)
                        if kt >= 2:
                            prep_tr(k + 1, kt - 2, tr_of[k + 1])
                if first_pair and k + 1 < E_eff:
                    prep_tr(k + 1, 6, tr_of[k + 1])
                    prep_tr(k + 1, 7, tr_of[k + 1])
                if diag:
                    svm = op("vector", lambda e: e.engine_nop(), "SV")
                    P.wait("tensor", "SV", svm)
                else:
                    P.wait("tensor", "SS", exp_done_ss)
                use_bank(4)
                use_bank(5)
                stv = None
                for kt in range(8):
                    for qh in range(2):
                        stv = mm(pb[4 + qh][0:128, 0:512], ones2d[:, :],
                                 pT[:, kt * 1024 + qh * 512: kt * 1024 + qh * 512 + 512],
                                 kt == 0, kt == 7,
                                 mark=(kt == 7 and qh == 1))
                P.wait("vector", "ST", stv)
                svl = None
                for qh in range(2):
                    sl = lsb[0:1, li * 1024 + qh * 512: li * 1024 + qh * 512 + 512]
                    if k == 0:
                        svl = op("vector", lambda e, qh=qh, sl=sl:
                                 e.tensor_copy(sl, pb[4 + qh][0:1, 0:512]), "SV")
                    else:
                        svl = op("vector", lambda e, qh=qh, sl=sl:
                                 e.tensor_add(sl, sl, pb[4 + qh][0:1, 0:512]), "SV")
                bank_read(4, "SV", svl)
                bank_read(5, "SV", svl)
                pvb = [(2, 3), (6, 7)]
                svpv = None
                for dt in range(8):
                    bk = pvb[dt % 2]
                    use_bank(bk[0])
                    use_bank(bk[1])
                    stv = None
                    for kt in range(8):
                        for qh in range(2):
                            stv = mm(pb[bk[qh]][:, 0:512],
                                     vtm[:, kt * 1024 + dt * 128: kt * 1024 + dt * 128 + 128],
                                     pT[:, kt * 1024 + qh * 512: kt * 1024 + qh * 512 + 512],
                                     kt == 0, kt == 7,
                                     mark=(kt == 7 and qh == 1))
                    P.wait("vector", "ST", stv)
                    for qh in range(2):
                        dst = Ob[:, dt * 1024 + qh * 512: dt * 1024 + qh * 512 + 512]
                        if k == 0:
                            svpv = op("vector", lambda e, dst=dst, b=bk[qh]:
                                      e.tensor_copy(dst, pb[b][:, 0:512]), "SV")
                        else:
                            svpv = op("vector", lambda e, dst=dst, b=bk[qh]:
                                      e.tensor_add(dst, dst, pb[b][:, 0:512]), "SV")
                        bank_read(bk[qh], "SV", svpv)
                P.wait("tensor", "SV", svpv)
                pv_done_st = P.bump("ST", 1)
                P.emit("tensor", lambda e: e.sem_inc(SEMS["ST"], 1))
                first_pair = False
            per = (len(decode_jobs) + E - 1) // E
            while dec_i < min(len(decode_jobs), (k + 1) * per):
                decode_halftile(*decode_jobs[dec_i])
                dec_i += 1
        while not KTRUNC and dec_i < len(decode_jobs):
            decode_halftile(*decode_jobs[dec_i])
            dec_i += 1
        if DEBUG and VAR == 0:
            P.wait("gpsimd", "SV", P.ctr["SV"])
            P.wait("gpsimd", "ST", P.ctr["ST"])
            odma(dbg["dbg_pt"][:, :], pT[:, :], sem="SG1")
            odma(dbg["dbg_q"][:, :], QA[:, :], sem="SG1")
            odma(dbg["dbg_o"][:, :], OA[:, :], sem="SG1")
            odma(dbg["dbg_l"][0:1, :], lsb[0:1, :], sem="SG1")
        var_final.append(P.snapshot())

    maxctr = {s: max(vf[0][s] for vf in var_final) for s in SEM_NAMES}
    for VAR in range(4):
        P.restore(var_final[VAR])
        P.cur = P.bodies[VAR]
        for s in SEM_NAMES:
            d = maxctr[s] - P.ctr[s]
            if d > 0:
                if s in ("SG0", "SG1"):
                    assert d % 16 == 0
                    for _ in range(d // 16):
                        gdma(lsb[0:1, 0:1], xs[0:1, 0:1], s)
                else:
                    P.emit(SEM_OWNER[s],
                           lambda e, sm=s, dd=d: e.sem_inc(SEMS[sm], dd))
    minwait = {e: {} for e in ENGS}
    for e in ENGS:
        keys = set()
        for vf in var_final:
            keys |= set(vf[1][e])
        for kk in keys:
            minwait[e][kk] = min(vf[1][e].get(kk, 0) for vf in var_final)
    bf_merged = {}
    for vf in var_final:
        for bi, (sem, val) in vf[2].items():
            cur = bf_merged.get(bi)
            if cur is None or (cur[0] == sem and val > cur[1]):
                bf_merged[bi] = (sem, val)
            elif cur[0] != sem:
                bf_merged[bi] = (sem, maxctr[sem])
    P.ctr = dict(maxctr)
    P.waited = minwait
    P.bank_free = bf_merged
    P.cur = P.post

    P.wait("sync", "ST", P.ctr["ST"])
    P.wait("sync", "SV", P.ctr["SV"])
    P.wait("sync", "SS", P.ctr["SS"])
    P.wait("scalar", "ST", P.ctr["ST"])
    P.wait("scalar", "SV", P.ctr["SV"])
    P.wait("gpsimd", "SG1", P.ctr["SG1"])
    P.wait("gpsimd", "ST", P.ctr["ST"])
    P.wait("gpsimd", "SV", P.ctr["SV"])
    P.wait("gpsimd", "SS", P.ctr["SS"])

    if not KTRUNC:
        sv_rl = op("vector", lambda e: e.reciprocal(lsb[0:1, :], lsb[0:1, :]), "SV")

        v_wo = gdma(wstage[:].rearrange("p (s c) -> p s c", c=1024),
                    wout[:, :].rearrange("(s p) c -> p s c", p=128), "SG0")
        P.wait("tensor", "SG0", v_wo)
        wo_tr = {"st": 0, "ss": 0, "sv": 0}
        for s in range(8):
            stv, esem, ev_v = pe_tr8(wstage, s * 1024, wxT, 1024, s * 128,
                                     (0, 1, 2, 3)[s % 4],
                                     ("scalar", "vector")[s % 2])
            wo_tr["st"] = stv
            wo_tr[esem.lower()] = max(wo_tr[esem.lower()], ev_v)
        ss_ga = pp_to_row(ppv2, 0, rowmm)
        P.wait("tensor", "SS", ss_ga)

        ss_garep = outer_rep(rowmm[0:1, :], foldrep)
        P.wait("vector", "SS", wo_tr["ss"])
        P.wait("vector", "SV", wo_tr["sv"])
        P.wait("vector", "SS", ss_garep)
        sv_fold = None
        for cc in range(CC):
            sv_fold = op("vector", lambda e, cc=cc:
                         e.tensor_mul(wxT[:, cc * 1024:(cc + 1) * 1024],
                                      wxT[:, cc * 1024:(cc + 1) * 1024],
                                      foldrep[:, :]), "SV")
        sv_bv = op("vector", lambda e: e.tensor_copy(bv_bf[:, :], ppv[:, 24:32]), "SV")
        P.wait("tensor", "SV", sv_fold)
        P.wait("tensor", "SV", sv_bv)
        use_bank(4)
        use_bank(5)
        stv = None
        for qh in range(2):
            for cc in range(CC):
                stv = mm(pb[4 + qh][0:1, 0:512], bv_bf[:, cc:cc + 1],
                         wxT[:, cc * 1024 + qh * 512: cc * 1024 + qh * 512 + 512],
                         cc == 0, cc == CC - 1,
                         mark=(cc == CC - 1 and qh == 1))
        P.wait("scalar", "ST", stv)
        ss_bvwo = None
        for qh in range(2):
            ss_bvwo = op("scalar", lambda e, qh=qh:
                         e.activation(rowfa[0:1, qh * 512:(qh + 1) * 512],
                                      pb[4 + qh][0:1, 0:512], AF.Copy), "SS")
            bank_read(4 + qh, "SS", ss_bvwo)
        v_ob = gdma(rowmm2[0:1, :], outbias[0:1, :], "SG0")
        P.wait("vector", "SS", ss_bvwo)
        P.wait("vector", "SG0", v_ob)

        op("vector", lambda e: e.tensor_mul(rowmm2[0:1, :], rowmm[0:1, :],
                                            rowmm2[0:1, :]))
        op("vector", lambda e: e.tensor_add(rowfa[0:1, :], rowfa[0:1, :],
                                            rowmm2[0:1, :]))
        sv_gab = op("vector", lambda e: e.tensor_copy(rowmm[0:1, :],
                                                      rowfa[0:1, :]), "SV")
        P.wait("tensor", "SV", sv_gab)
        ss_gabrep = outer_rep(rowmm[0:1, :], gab_rep)

        ss_r = pp_to_row(ppv2, 16, rowmm)
        P.wait("tensor", "SS", ss_r)
        ss_cb1 = outer_rep(rowmm[0:1, :], cb1_rep)
        ss_r = pp_to_row(ppv2, 8, rowmm)
        P.wait("tensor", "SS", ss_r)
        ss_sb = outer_rep(rowmm[0:1, :], sb_rep)
        ss_r = pp_to_row(ppv2, 24, rowmm)
        P.wait("gpsimd", "SV", sv_gab)
        v_fb = gdma(rowmm2[0:1, :], fbias[0:1, :], "SG0")
        P.wait("vector", "SS", ss_r)
        P.wait("vector", "SG0", v_fb)
        sv_gbb = op("vector", lambda e: e.tensor_mul(rowmm2[0:1, :], rowmm[0:1, :],
                                                     rowmm2[0:1, :]), "SV")
        P.wait("tensor", "SV", sv_gbb)
        ss_gbbrep = outer_rep(rowmm2[0:1, :], gbb_rep)

        P.wait("gpsimd", "ST", wo_tr["st"])
        v_w8 = gdma(wstage[:].rearrange("p (s c) -> p s c", c=1024),
                    w8dec[:, :].rearrange("(s p) c -> p s c", p=128), "SG0")
        P.wait("tensor", "SG0", v_w8)
        w8_tr = {"st": 0, "ss": 0, "sv": 0}
        for s in range(8):
            stv, esem, ev_v = pe_tr8(wstage, s * 1024, w8T, 1024, s * 128,
                                     (0, 1, 2, 3)[s % 4],
                                     ("scalar", "vector")[s % 2])
            w8_tr["st"] = stv
            w8_tr[esem.lower()] = max(w8_tr[esem.lower()], ev_v)
        ss_gbrep = outer_rep(rowmm[0:1, :], foldrep)
        P.wait("vector", "SS", w8_tr["ss"])
        P.wait("vector", "SV", w8_tr["sv"])
        P.wait("vector", "SS", ss_gbrep)
        sv_w8fold = None
        for cc in range(CC):
            sv_w8fold = op("vector", lambda e, cc=cc:
                           e.tensor_mul(w8T[:, cc * 1024:(cc + 1) * 1024],
                                        w8T[:, cc * 1024:(cc + 1) * 1024],
                                        foldrep[:, :]), "SV")
        if DEBUG:
            P.wait("gpsimd", "SV", sv_w8fold)
            odma(dbg["dbg_w8t"][:, :], w8T[:, :], sem="SG1")

        out_dma = {}
        x2_sp = {}
        trq = {}
        ffn_stv = {}
        xin_sv = {}
        qs_ss = {}
        q_lnv = {}
        rl_ss = [0]

        def rl_prep(li, Ob):
            P.wait("vector", "SV", sv_rl)
            P.wait("vector", "ST", P.ctr["ST"])
            lslice = lsb[0:1, li * 1024:(li + 1) * 1024]
            op("vector", lambda e, lslice=lslice:
               e.tensor_copy(rowmm[0:1, :], lslice))
            op("vector", lambda e, lslice=lslice:
               e.tensor_sub(rowfa[0:1, :], lslice, rowmm[0:1, :]))
            sv_lo = op("vector", lambda e: e.tensor_copy(rowmm2[0:1, :],
                                                         rowfa[0:1, :]), "SV")
            P.wait("tensor", "SV", sv_lo)
            use_bank(4)
            use_bank(5)
            stv = None
            for qh in range(2):
                mm(pb[4 + qh][:, 0:512], ones1r[0:1, 0:128],
                   rowmm[0:1, qh * 512:(qh + 1) * 512], True, False)
                stv = mm(pb[4 + qh][:, 0:512], ones1r[0:1, 0:128],
                         rowmm2[0:1, qh * 512:(qh + 1) * 512], False, True,
                         mark=(qh == 1))
            P.wait("scalar", "ST", stv)
            ss_rep = None
            for qh in range(2):
                ss_rep = op("scalar", lambda e, qh=qh:
                            e.activation(rlrep[:, qh * 512:(qh + 1) * 512],
                                         pb[4 + qh][:, 0:512], AF.Copy), "SS")
                bank_read(4 + qh, "SS", ss_rep)
            rl_ss[0] = ss_rep
            P.wait("vector", "SS", ss_rep)
            sv_on = None
            for dt in range(8):
                sv_on = op("vector", lambda e, dt=dt, Ob=Ob:
                           e.tensor_mul(Ob[:, dt * 1024:(dt + 1) * 1024],
                                        Ob[:, dt * 1024:(dt + 1) * 1024],
                                        rlrep[:, :]), "SV")
            return sv_on

        gb4 = [2, 3, 6, 7]
        v_x_cur = {}

        def outproj_tile(li, Ob, qt, sv_on):
            x2b = x2buf if li == 0 else x2bufB
            if qt == 0:
                P.wait("tensor", "SV", sv_on)
            if qt % 2 == 0:
                qtr = qt // 2
                pv = max(xin_sv.get((li, qt - 2), 0), xin_sv.get((li, qt - 1), 0),
                         xin_sv.get((li - 1, 6), 0), xin_sv.get((li - 1, 7), 0))
                P.wait("sync", "SV", pv)
                v_x_cur[li] = dma("sync",
                                  xin[:].rearrange("p (s c) -> p s c", c=1024),
                                  xq[li * TB + qtr * 256: li * TB + (qtr + 1) * 256, :]
                                  .rearrange("(s p) c -> p s c", p=128))
            qt2 = qt % 2
            b0, b1 = ((0, 1), (4, 5))[qt % 2]
            use_bank(b0)
            use_bank(b1)
            stv = None
            for dc in range(CC):
                for ch, bi in ((0, b0), (1, b1)):
                    stv = mm(pb[bi][:, 0:512],
                             Ob[:, dc * 1024 + qt * 128: dc * 1024 + qt * 128 + 128],
                             wxT[:, dc * 1024 + ch * 512: dc * 1024 + ch * 512 + 512],
                             dc == 0, dc == CC - 1,
                             mark=(dc == CC - 1 and ch == 1))
            P.wait("vector", "ST", stv)
            P.wait("vector", "SD", v_x_cur[li])
            svv = 0
            for ch, bi in ((0, b0), (1, b1)):
                dst = x2b[:, qt * 1024 + ch * 512: qt * 1024 + ch * 512 + 512]
                svv = op("vector", lambda e, dst=dst, bi=bi, qt2=qt2, ch=ch:
                         e.scalar_tensor_tensor(
                             dst, pb[bi][:, 0:512], 1.0,
                             xin[:, qt2 * 1024 + ch * 512: qt2 * 1024 + ch * 512 + 512],
                             OP.mult, OP.add), "SV")
                bank_read(bi, "SV", svv)
            xin_sv[(li, qt)] = svv
            P.wait("gpsimd", "SV", svv)
            P.wait("gpsimd", "SS", ss_gabrep)
            spv = 0
            for ch in range(2):
                dst = x2b[:, qt * 1024 + ch * 512: qt * 1024 + ch * 512 + 512]
                spv = op("gpsimd", lambda e, dst=dst, ch=ch:
                         e.tensor_add(dst, dst,
                                      gab_rep[:, ch * 512:(ch + 1) * 512]), "SP")
            x2_sp[(li, qt)] = spv

        def qa(li, qt):
            j = li * 8 + qt
            x2b = x2buf if li == 0 else x2bufB
            x2t = x2b[:, qt * 1024:(qt + 1) * 1024]
            P.wait("vector", "SP", x2_sp[(li, qt)])
            pv = trq.get(j - 2)
            if pv:
                P.wait("scalar", "SD", pv["sync"])
                P.wait("scalar", "SDS", pv["scalar"])
            if j - 2 in q_lnv:
                P.wait("vector", *q_lnv[j - 2])
            q_lnv[j] = ln_tile(x2t, qt, sqe)

        def qb(li, qt):
            j = li * 8 + qt
            qs0_ = qs0 if j % 2 == 0 else qs0b
            h2t = lntm[:, (qt % 2) * 1024:(qt % 2) * 1024 + 1024]
            P.wait("vector", *q_lnv[j])
            op("vector", lambda e, h2t=h2t: e.tensor_mul(h2t, h2t, cb1_rep))
            op("vector", lambda e, h2t=h2t: e.tensor_add(h2t, h2t, sb_rep))
            c0 = (qt % 2) * 8
            am = stats[:, c0 + 6:c0 + 7]
            rq = stats[:, c0 + 7:c0 + 8]
            op("vector", lambda e, h2t=h2t, am=am:
               e.reduce_max(am, h2t, AX.X, apply_absolute_value=True))
            vnop()
            op("vector", lambda e, am=am:
               e.tensor_scalar(am, am, 1e-12, None, OP.max))
            vnop()
            op("vector", lambda e, am=am, rq=rq: e.reciprocal(rq, am))
            vnop()
            op("vector", lambda e, rq=rq:
               e.tensor_scalar(rq, rq, 448.0, None, OP.mult))
            vnop()
            sv_dq = op("vector", lambda e, am=am, qt=qt:
                       e.tensor_scalar(ppv[:, 40 + qt:41 + qt], am,
                                       ppv[:, 48:49], 1.0 / 448.0,
                                       OP.mult, OP.mult), "SV")
            P.wait("scalar", "SV", sv_dq)
            ss_v = op("scalar", lambda e, h2t=h2t, rq=rq, qs0_=qs0_:
                      e.mul(qs0_[:, :], h2t, rq), "SS")
            qs_ss[(li, qt)] = ss_v

        def q3(li, qt):
            j = li * 8 + qt
            qs0_, qs1_, qs2_ = (qs0, qs1, qs2) if j % 2 == 0 else (qs0b, qs1b, qs2b)
            if li == 1:
                pa = trq[qt]
                P.wait("vector", "SD", pa["sync"])
                P.wait("vector", "SDS", pa["scalar"])
            P.wait("vector", "SS", qs_ss[(li, qt)])
            op("vector", lambda e, qs0_=qs0_, qs1_=qs1_:
               e.tensor_scalar(qs1_[:, :], qs0_[:, :], C1, None, OP.mult))
            op("vector", lambda e, qs0_=qs0_, qs1_=qs1_, qs2_=qs2_:
               e.tensor_sub(qs2_[:, :], qs1_[:, :], qs0_[:, :]))
            op("vector", lambda e, qs1_=qs1_, qs2_=qs2_:
               e.tensor_sub(qs1_[:, :], qs1_[:, :], qs2_[:, :]))
            sv_q8 = 0
            for hh in range(2):
                sl = slice(hh * 512, (hh + 1) * 512)
                op("vector", lambda e, sl=sl, qs0_=qs0_, qs2_=qs2_: e.tensor_mul(
                    qs2_[:, sl], qs0_[:, sl], qs0_[:, sl]))
                op("vector", lambda e, sl=sl, qs2_=qs2_: e.tensor_scalar(
                    msku[:, :], qs2_[:, sl], float(2.0 ** -12), None, OP.is_ge))
                op("vector", lambda e, sl=sl, qs0_=qs0_, qs2_=qs2_: e.tensor_scalar(
                    qs2_[:, sl], qs0_[:, sl], MAGIC, -MAGIC, OP.add, OP.add))
                sv_q8 = op("vector", lambda e, qt=qt, sl=sl, hh=hh, qs1_=qs1_, qs2_=qs2_:
                           e.select(q8tm[:, qt * 1024 + hh * 512: qt * 1024 + (hh + 1) * 512],
                                    msku[:, :], qs1_[:, sl], qs2_[:, sl]), "SV")
            if DEBUG and li == 0 and qt == 7:
                P.wait("gpsimd", "SV", sv_q8)
                odma(dbg["dbg_q8"][:, :], q8tm[:, :], sem="SG1")
            P.wait("sync", "SV", sv_q8)
            if li == 1:
                P.wait("sync", "ST", ffn_stv[(0, qt)])
            vals = {"sync": 0, "scalar": 0}
            for cc in range(CC):
                vals["sync"] = dma("sync",
                                   q8fm[:, cc * 1024 + qt * 128: cc * 1024 + qt * 128 + 128],
                                   q8tm[:, qt * 1024 + cc * 128: qt * 1024 + cc * 128 + 128],
                                   transpose=True)
            trq[j] = vals

        def ffn_tile(li, qt):
            x2b = x2buf if li == 0 else x2bufB
            b0, b1 = gb4[(qt % 2) * 2], gb4[(qt % 2) * 2 + 1]
            use_bank(b0)
            use_bank(b1)
            P.wait("tensor", "SD", trq[li * 8 + qt]["sync"])
            P.wait("tensor", "SDS", trq[li * 8 + qt]["scalar"])
            if li == 0 and qt == 0:
                P.wait("tensor", "SV", sv_w8fold)
            stv = None
            for cc in range(CC):
                for nh, bi in ((0, b0), (1, b1)):
                    stv = mm(pb[bi][:, 0:512],
                             q8fm[:, cc * 1024 + qt * 128: cc * 1024 + qt * 128 + 128],
                             w8T[:, cc * 1024 + nh * 512: cc * 1024 + nh * 512 + 512],
                             cc == 0, cc == CC - 1,
                             mark=(cc == CC - 1 and nh == 1))
            ffn_stv[(li, qt)] = stv
            ostg = ost0 if qt % 2 == 0 else ost1
            prev = out_dma.get((li, qt - 2)) or out_dma.get((li - 1, qt + 6))
            if prev:
                P.wait("vector", prev[1], prev[0])
            P.wait("vector", "ST", stv)
            stt = 0
            for nh, bi in ((0, b0), (1, b1)):
                stt = op("vector", lambda e, bi=bi, qt=qt, nh=nh, ostg=ostg:
                         e.scalar_tensor_tensor(
                             ostg[:, nh * 512:(nh + 1) * 512], pb[bi][:, 0:512],
                             ppv[:, 40 + qt:41 + qt],
                             x2b[:, qt * 1024 + nh * 512: qt * 1024 + nh * 512 + 512],
                             OP.mult, OP.add), "SV")
                bank_read(bi, "SV", stt)
            P.wait("gpsimd", "SV", stt)
            P.wait("gpsimd", "SS", ss_gbbrep)
            for nh in range(2):
                op("gpsimd", lambda e, nh=nh, ostg=ostg:
                   e.tensor_add(ostg[:, nh * 512:(nh + 1) * 512],
                                ostg[:, nh * 512:(nh + 1) * 512],
                                gbb_rep[:, nh * 512:(nh + 1) * 512]), "SP")
            out_dma[(li, qt)] = odma(
                out[li * TB + qt * 128: li * TB + (qt + 1) * 128, :], ostg[:, :],
                sem="SOUT" if qt % 2 == 0 else "SOUT1")

    if not KTRUNC:
        sv_on_a = rl_prep(0, OA)
        for qt in range(8):
            outproj_tile(0, OA, qt, sv_on_a)
        sv_on_b = rl_prep(1, OB)
        for qt in range(8):
            qa(0, qt)
            if qt >= 1:
                qb(0, qt - 1)
            if qt >= 2:
                q3(0, qt - 2)
            outproj_tile(1, OB, qt, sv_on_b)
            if qt >= 4:
                ffn_tile(0, qt - 4)
        for qt in range(8):
            qa(1, qt)
            qb(1, qt - 1) if qt >= 1 else qb(0, 7)
            if qt == 0:
                q3(0, 6)
            elif qt == 1:
                q3(0, 7)
            else:
                q3(1, qt - 2)
            if 1 <= qt <= 4:
                ffn_tile(0, qt + 3)
            elif qt >= 5:
                ffn_tile(1, qt - 5)
        qb(1, 7)
        q3(1, 6)
        q3(1, 7)
        for m in (3, 4, 5, 6, 7):
            ffn_tile(1, m)
        P.wait("gpsimd", "SOUT", P.ctr["SOUT"])
        P.wait("gpsimd", "SOUT1", P.ctr["SOUT1"])
    else:
        P.wait("gpsimd", "ST", P.ctr["ST"])
        P.wait("gpsimd", "SV", P.ctr["SV"])
        P.wait("gpsimd", "SS", P.ctr["SS"])
        odma(dbg["dbg_lnfm"][:, :], lnfm[:, :], sem="SG1")
        odma(dbg["dbg_k"][:, :], kfm[:, :], sem="SG1")
        odma(dbg["dbg_v"][:, :], vtm[:, :], sem="SG1")
        odma(dbg["dbg_q"][:, :], QA[:, :], sem="SG1")
        odma(dbg["dbg_pt"][:, :], pT[:, :], sem="SG1")
        odma(dbg["dbg_l"][0:1, :], lsb[0:1, :], sem="SG1")
        odma(dbg["dbg_o"][:, :], OA[:, :], sem="SG1")
        odma(dbg["dbg_x2"][:, :], OB[:, :], sem="SG1")
        P.wait("gpsimd", "SG1", P.ctr["SG1"])

    with nc.Block() as blk:
        for eng in ENGS:
            getattr(blk, eng)(lambda e, _o=P.pre[eng]: [f(e) for f in _o])
    engs = [nc.tensor, nc.vector, nc.scalar, nc.sync, nc.gpsimd]
    idx = []
    for e in engs:
        e.wait_ge(SEMS["SD"], 16)
        idx.append(e.value_load(in_tensor=vid_s[0:1, 0:1]))
    for i in nc.Switch(engines=engs, index=idx, n=4):
        with nc.Block() as blk2:
            for eng in ENGS:
                getattr(blk2, eng)(lambda e, _o=P.bodies[i][eng]: [f(e) for f in _o])
    with nc.Block() as blk3:
        for eng in ENGS:
            getattr(blk3, eng)(lambda e, _o=P.post[eng]: [f(e) for f in _o])
    return nc


_NC_CACHE = {}


def _get_nc():
    if "nc" not in _NC_CACHE:
        _NC_CACHE["nc"] = build_nc()
    return _NC_CACHE["nc"]


def make_in_maps(L0_attn_x, L0_attn_qkv_w, L0_attn_out_w, L0_attn_out_b,
                 L0_ffn_weight, L0_ffn_weight_scale, L0_ffn_bias, L0_c, L0_wmod):
    x = np.ascontiguousarray(np.asarray(L0_attn_x, dtype=np.float32))
    wqkv_a = np.ascontiguousarray(np.asarray(L0_attn_qkv_w, dtype=np.float32))
    wout_a = np.ascontiguousarray(np.asarray(L0_attn_out_w, dtype=np.float32))
    outb = np.ascontiguousarray(np.asarray(L0_attn_out_b, dtype=np.float32)).reshape(1, D)
    w8u_a = np.ascontiguousarray(np.asarray(L0_ffn_weight).view(np.uint8))
    ws = np.asarray(L0_ffn_weight_scale, dtype=np.float32).reshape(1, 1)
    fb = np.ascontiguousarray(np.asarray(L0_ffn_bias, dtype=np.float32)).reshape(1, D)
    cc_ = np.ascontiguousarray(np.asarray(L0_c, dtype=np.float32))
    wm = np.ascontiguousarray(np.asarray(L0_wmod, dtype=np.float32))
    u = np.arange(1920)[None, :]
    r = np.arange(128)[:, None]
    stripv = np.ascontiguousarray(((u - 896) >= r).astype(ml_dtypes.bfloat16))
    in_maps = []
    for c in range(8):
        b, i = c // 4, c % 4
        xb = x[b]
        xqv = np.ascontiguousarray(np.concatenate(
            [xb[i * TB:(i + 1) * TB], xb[(7 - i) * TB:(8 - i) * TB]], axis=0))
        in_maps.append({
            "vid": np.array([[i]], dtype=np.int32),
            "identd": np.eye(128, dtype=np.float32).astype(ml_dtypes.bfloat16),
            "xs": xb, "xq": xqv, "wqkv": wqkv_a, "wout": wout_a,
            "w8u": w8u_a, "wscale": ws, "fbias": fb, "outbias": outb,
            "cvec": np.ascontiguousarray(cc_[b]).reshape(1, D),
            "wmod": wm, "stripd": stripv,
        })
    return in_maps


def kernel(L0_attn_x, L0_attn_qkv_w, L0_attn_out_w, L0_attn_out_b,
           L0_ffn_weight, L0_ffn_weight_scale, L0_ffn_bias, L0_c, L0_wmod):
    nc = _get_nc()
    in_maps = make_in_maps(L0_attn_x, L0_attn_qkv_w, L0_attn_out_w,
                           L0_attn_out_b, L0_ffn_weight, L0_ffn_weight_scale,
                           L0_ffn_bias, L0_c, L0_wmod)
    res = run_bass_kernel_spmd(nc, in_maps, core_ids=list(range(8)))
    outv = np.empty((2, 8 * TB, D), dtype=np.float32)
    for c in range(8):
        b, i = c // 4, c % 4
        r = res.results[c]["out"]
        outv[b, i * TB:(i + 1) * TB] = r[0:TB]
        outv[b, (7 - i) * TB:(8 - i) * TB] = r[TB:2 * TB]
    return outv


if __name__ == "__main__":
    nc = build_nc()
    n_instr = sum(len(b.instructions) for b in nc.m.functions[0].blocks) \
        if hasattr(nc, "m") else -1
    print("build ok")


# version 21 (final)
# speedup vs baseline: 1.0386x; 1.0386x over previous
import numpy as np
import ml_dtypes
import concourse.bass as bass
import concourse.mybir as mybir
from concourse.bass_utils import run_bass_kernel_spmd

F32 = mybir.dt.float32
BF16 = mybir.dt.bfloat16
U8 = mybir.dt.uint8
I32 = mybir.dt.int32
AF = mybir.ActivationFunctionType
OP = mybir.AluOpType
AX = mybir.AxisListType

TB = 1024
NT = 8
D = 1024
CC = 8
MSHIFT = 20.0
C1 = float(np.float32(2.0 ** 20 + 1))
MAGIC = float(np.float32(1.5 * 2.0 ** 14))
EPS_LN = 1e-5
DEBUG = False
import os
KTRUNC = int(os.environ.get("KTRUNC", "0"))

ENGS = ("sync", "scalar", "vector", "tensor", "gpsimd")
SEM_NAMES = ("SD", "SDS", "SG0", "SG1", "SG2", "SG3", "ST", "SV", "SS", "SP", "SOUT", "SOUT1")
SEM_OWNER = {"SD": "sync", "SDS": "scalar", "SG0": "gpsimd", "SG1": "gpsimd", "SG2": "gpsimd", "SG3": "gpsimd",
             "ST": "tensor", "SV": "vector", "SS": "scalar", "SP": "gpsimd",
             "SOUT": "gpsimd",
             "SOUT1": "gpsimd"}


def _sz(dtype):
    return {F32: 4, BF16: 2, U8: 1, I32: 4}[dtype]


class Prog:
    def __init__(self):
        self.pre = {e: [] for e in ENGS}
        self.bodies = [{e: [] for e in ENGS} for _ in range(4)]
        self.post = {e: [] for e in ENGS}
        self.cur = self.pre
        self.ctr = {s: 0 for s in SEM_NAMES}
        self.waited = {e: {} for e in ENGS}
        self.bank_free = {}

    def snapshot(self):
        return (dict(self.ctr), {e: dict(w) for e, w in self.waited.items()},
                dict(self.bank_free))

    def restore(self, s):
        self.ctr = dict(s[0])
        self.waited = {e: dict(w) for e, w in s[1].items()}
        self.bank_free = dict(s[2])

    def emit(self, eng, fn):
        self.cur[eng].append(fn)

    def wait(self, eng, sem, val):
        if val is None or val <= 0:
            return
        if self.waited[eng].get(sem, -1) >= val:
            return
        self.waited[eng][sem] = val
        self.emit(eng, lambda e, s=sem, v=val: e.wait_ge(SEMS[s], v))

    def bump(self, sem, n):
        self.ctr[sem] += n
        return self.ctr[sem]


SEMS = {}
P = Prog()


def build_nc():
    global P, SEMS
    P = Prog()
    SEMS = {}
    nc = bass.Bass("TRN2", target_bir_lowering=False,
                   detect_race_conditions=False)

    vid = nc.dram_tensor("vid", [1, 1], I32, kind="ExternalInput")
    xs = nc.dram_tensor("xs", [8 * TB, D], F32, kind="ExternalInput")
    xq = nc.dram_tensor("xq", [2 * TB, D], F32, kind="ExternalInput")
    wqkv = nc.dram_tensor("wqkv", [3 * D, D], F32, kind="ExternalInput")
    wout = nc.dram_tensor("wout", [D, D], F32, kind="ExternalInput")
    w8u = nc.dram_tensor("w8u", [D, D], U8, kind="ExternalInput")
    wscale = nc.dram_tensor("wscale", [1, 1], F32, kind="ExternalInput")
    fbias = nc.dram_tensor("fbias", [1, D], F32, kind="ExternalInput")
    outbias = nc.dram_tensor("outbias", [1, D], F32, kind="ExternalInput")
    cvec = nc.dram_tensor("cvec", [1, D], F32, kind="ExternalInput")
    wmod = nc.dram_tensor("wmod", [D, 6 * D], F32, kind="ExternalInput")
    stripd = nc.dram_tensor("stripd", [128, 1920], BF16, kind="ExternalInput")
    out = nc.dram_tensor("out", [2 * TB, D], F32, kind="ExternalOutput")
    w8dec = nc.dram_tensor("w8dec", [D, D], BF16, kind="Internal")
    identd = nc.dram_tensor("identd", [128, 128], BF16, kind="ExternalInput")
    dbg = {}
    if DEBUG:
        for nm, shape, dt in [("dbg_lnfm", [128, 8192], BF16),
                              ("dbg_k", [128, 8192], BF16),
                              ("dbg_v", [128, 8192], BF16),
                              ("dbg_q", [128, 8192], BF16),
                              ("dbg_pt", [128, 8192], BF16),
                              ("dbg_l", [1, 2048], F32),
                              ("dbg_row", [1, 1024], BF16),
                              ("dbg_row2", [1, 1024], BF16),
                              ("dbg_rowf", [1, 1024], F32),
                              ("dbg_h2", [128, 1024], F32),
                              ("dbg_vq", [128, 1024], F32),
                              ("dbg_o", [128, 8192], BF16),
                              ("dbg_x2", [128, 8192], BF16),
                              ("dbg_q8", [128, 8192], BF16),
                              ("dbg_w8t", [128, 8192], BF16)]:
            dbg[nm] = nc.dram_tensor(nm, shape, dt, kind="ExternalOutput")

    base0 = int(nc.sbuf_base)
    off = [base0]
    lim = int(nc.sbuf_top)

    def at(name, shape, dtype, place=None):
        if place is None:
            o = (off[0] + 31) & ~31
            off[0] = o + int(np.prod(shape[1:])) * _sz(dtype)
        else:
            o = place
        return nc.alloc_sbuf_tensor_at(name, shape, dtype, offset=o), o

    wkvT, o_wkv = at("wkvT", [128, 16384], BF16)
    wxT, o_wx = at("wxT", [128, 8192], BF16)
    QA, o_qa = at("QA", [128, 8192], BF16)
    QB, o_qb = at("QB", [128, 8192], BF16)
    OA, o_oa = at("OA", [128, 8192], BF16)
    OB, _ = at("OB", [128, 8192], BF16)
    xin, _ = at("xin", [128, 2048], F32)
    lntm, _ = at("lntm", [128, 2048], BF16)
    lnfm, o_lf = at("lnfm", [128, 8192], BF16)
    kfm, o_kf = at("kfm", [128, 8192], BF16)
    vtm, o_vt = at("vtm", [128, 8192], BF16)
    pT, _ = at("pT", [128, 8192], BF16)
    strip, _ = at("strip", [128, 1920], BF16, place=o_wx)
    wmb0, _ = at("wmb0", [128, 4096], BF16, place=o_kf)
    wmb1, _ = at("wmb1", [128, 4096], BF16, place=o_kf + 8192)
    qs0, _ = at("qs0", [128, 1024], F32, place=o_kf)
    qs1, _ = at("qs1", [128, 1024], F32, place=o_kf + 4096)
    qs2, _ = at("qs2", [128, 1024], F32, place=o_kf + 8192)
    ost0, _ = at("ost0", [128, 1024], F32, place=o_wkv)
    ost1, _ = at("ost1", [128, 1024], F32, place=o_wkv + 4096)
    reps, _ = at("reps", [128, 4096], BF16, place=o_vt)
    rlrep, _ = at("rlrep", [128, 1024], F32, place=o_vt + 8192)
    foldrep, _ = at("foldrep", [128, 1024], BF16, place=o_vt + 12288)
    x2buf, _ = at("x2buf", [128, 8192], BF16, place=o_lf)
    x2bufB, _ = at("x2bufB", [128, 8192], BF16, place=o_oa)
    sqv, _ = at("sqv", [128, 2048], F32, place=o_vt)
    sqx, _ = at("sqx", [128, 2048], F32, place=o_wx + 4096)
    sqe, _ = at("sqe", [128, 2048], F32, place=o_wkv + 8192)
    qs0b, _ = at("qs0b", [128, 1024], F32, place=o_wkv + 16384)
    qs1b, _ = at("qs1b", [128, 1024], F32, place=o_wkv + 20480)
    qs2b, _ = at("qs2b", [128, 1024], F32, place=o_wkv + 24576)
    w8T = pT
    q8tm = QA
    q8fm = QB
    wstage = QB
    wstage2 = QA
    vid_s, _ = at("vid_s", [1, 1], I32)
    stats, _ = at("stats", [128, 16], F32)
    ppv, _ = at("ppv", [128, 49], F32)
    ppv_bf, _ = at("ppv_bf", [128, 8], BF16)
    s_pp, _ = at("s_pp", [128, 8], BF16)
    bv_bf, _ = at("bv_bf", [128, 8], BF16)
    ones128, _ = at("ones128", [128, 1], BF16)
    negm, _ = at("negm", [128, 1], F32)
    ones1r, _ = at("ones1r", [1, 128], BF16)
    dect = [at(f"dec{j}", [128, 256], BF16)[0] for j in range(4)]
    u8in, _ = at("u8in", [128, 256], U8)
    msku, o_msku = at("msku", [128, 512], U8)
    ones2d, _ = at("ones2d", [128, 128], BF16, place=o_msku + 256)
    rowmm, _ = at("rowmm", [1, 1024], BF16)
    rowmm2, _ = at("rowmm2", [1, 1024], BF16)
    rowfa, _ = at("rowfa", [1, 1024], F32)
    lsb, _ = at("lsb", [1, 2048], F32)
    ppv2, _ = at("ppv2", [128, 32], F32)
    ident, _ = at("ident", [128, 128], BF16)
    assert off[0] <= lim, f"SBUF over budget: {off[0]} > {lim}"

    cb1_rep = reps[:, 0:1024]
    sb_rep = reps[:, 1024:2048]
    gab_rep = reps[:, 2048:3072]
    gbb_rep = reps[:, 3072:4096]

    pb = [nc.alloc_psum_tensor(f"pb{j}", [128, 512], F32) for j in range(8)]

    for s in SEM_NAMES:
        SEMS[s] = nc.alloc_semaphore(s)



    def dma(q, dst, src, transpose=False):
        sem = "SD" if q == "sync" else "SDS"
        v = P.bump(sem, 16)
        P.emit(q, lambda e, d=dst, s=src, t=transpose, sm=sem:
               e.dma_start(d, s, transpose=t).then_inc(SEMS[sm], 16))
        return v

    def gdma(dst, src, sem="SG0"):
        prev = P.ctr[sem]
        if prev:
            P.wait("gpsimd", sem, prev)
        v = P.bump(sem, 16)
        P.emit("gpsimd", lambda e, d=dst, s=src, sm=sem:
               e.dma_start(d, s).then_inc(SEMS[sm], 16))
        return v

    def odma(dst, src, sem="SOUT"):
        prev = P.ctr[sem]
        if prev:
            P.wait("gpsimd", sem, prev)
        v = P.bump(sem, 16)
        P.emit("gpsimd", lambda e, d=dst, s=src, sm=sem:
               e.dma_start(d, s).then_inc(SEMS[sm], 16))
        return v, sem

    def vnop():
        P.emit("vector", lambda e: e.engine_nop())
        P.emit("vector", lambda e: e.engine_nop())
        P.emit("vector", lambda e: e.engine_nop())

    def op(eng, fn, sem=None):
        if sem is None:
            P.emit(eng, fn)
            return None
        v = P.bump(sem, 1)
        P.emit(eng, lambda e, f=fn, sm=sem: f(e).then_inc(SEMS[sm], 1))
        return v

    def mm(outap, lhsT, rhs, start, stop, mark=False):
        if mark:
            v = P.bump("ST", 1)
            P.emit("tensor", lambda e, o=outap, l=lhsT, r=rhs, st=start, sp=stop:
                   e.matmul(o, l, r, start=st, stop=sp).then_inc(SEMS["ST"], 1))
            return v
        P.emit("tensor", lambda e, o=outap, l=lhsT, r=rhs, st=start, sp=stop:
               e.matmul(o, l, r, start=st, stop=sp))
        return None

    def use_bank(i):
        bf = P.bank_free.get(i)
        if bf is not None:
            P.wait("tensor", bf[0], bf[1])

    def bank_read(i, sem, val):
        P.bank_free[i] = (sem, val)

    def ln_tile(xtile, slot, sq):
        c0 = (slot % 2) * 8
        s1 = stats[:, c0:c0 + 1]
        s2 = stats[:, c0 + 1:c0 + 2]
        rstd = stats[:, c0 + 2:c0 + 3]
        mean = stats[:, c0 + 4:c0 + 5]
        var = stats[:, c0 + 5:c0 + 6]
        dst = lntm[:, (slot % 2) * 1024:(slot % 2) * 1024 + 1024]
        sqs = sq[:, (slot % 2) * 1024:(slot % 2) * 1024 + 1024]
        op("vector", lambda e: e.reduce_sum(s1, xtile, AX.X))
        op("vector", lambda e: e.tensor_mul(sqs, xtile, xtile))
        op("vector", lambda e: e.reduce_sum(s2, sqs, AX.X))
        vnop()
        op("vector", lambda e: e.tensor_scalar(mean, s1, 1.0 / D, None, OP.mult))
        op("vector", lambda e: e.tensor_scalar(s2, s2, 1.0 / D, EPS_LN,
                                               OP.mult, OP.add))
        op("vector", lambda e: e.scalar_tensor_tensor(var, mean, -1.0, mean,
                                                      OP.mult, OP.mult))
        vnop()
        op("vector", lambda e: e.tensor_add(var, var, s2))
        vnop()
        sv2 = op("vector", lambda e: e.tensor_scalar(sqs, xtile, mean,
                                                     None, OP.subtract), "SV")
        P.wait("scalar", "SV", sv2)
        op("scalar", lambda e: e.activation(rstd, var, AF.Ln))
        op("scalar", lambda e: e.activation(rstd, rstd, AF.Exp, scale=-0.5))
        ss3 = op("scalar", lambda e: e.activation(dst, sqs, AF.Identity,
                                                  scale=rstd), "SS")
        return ("SS", ss3)

    def pe_transpose_tile(t_idx, dstbuf, bank, ev):
        so = (t_idx % 2) * 1024
        stv, esem, ev_v = pe_tr8(lntm, so, dstbuf, 1024, t_idx * 128, bank, ev)
        return {"st": stv, "ss": ev_v if esem == "SS" else 0,
                "sv": ev_v if esem == "SV" else 0}

    def pe_tr8(src, src_base, dst, dst_stride, dst_off, bank, ev="scalar"):
        use_bank(bank)
        pbf = pb[bank].ap().bitcast(BF16)
        for cc in range(7):
            P.emit("tensor", lambda e, pbf=pbf, cc=cc, src=src, sb=src_base:
                   e.transpose(pbf[:, cc * 128:(cc + 1) * 128],
                               src[:, sb + cc * 128: sb + cc * 128 + 128],
                               ident[:, :]))
        stv = P.bump("ST", 1)
        P.emit("tensor", lambda e, pbf=pbf, src=src, sb=src_base:
               e.transpose(pbf[:, 896:1024], src[:, sb + 896: sb + 1024],
                           ident[:, :]).then_inc(SEMS["ST"], 1))
        sem = {"scalar": "SS", "vector": "SV"}[ev]
        P.wait(ev, "ST", stv)
        dv = (dst[:, 0:8 * dst_stride]
              .rearrange("p (cc c) -> p cc c", c=dst_stride)
              [:, :, dst_off:dst_off + 128])
        sv = pbf[:, 0:1024].rearrange("p (cc c) -> p cc c", c=128)
        if ev == "scalar":
            ev_v = op(ev, lambda e, dv=dv, sv=sv:
                      e.activation(dv, sv, AF.Copy), sem)
        else:
            ev_v = op(ev, lambda e, dv=dv, sv=sv:
                      e.tensor_copy(dv, sv), sem)
        bank_read(bank, sem, ev_v)
        return stv, sem, ev_v

    def row_to_pp(rowap, dsttile, dstcol, bias=0.0):
        last = None
        for t in range(8):
            bi = 4 + (t % 2)
            use_bank(bi)
            stv = mm(pb[bi][0:128, 0:1], rowap[0:1, t * 128:(t + 1) * 128],
                     ones1r[0:1, 0:1], True, True, mark=True)
            P.wait("scalar", "ST", stv)
            last = op("scalar", lambda e, t=t, bi=bi:
                      e.activation(dsttile[:, dstcol + t:dstcol + t + 1],
                                   pb[bi][0:128, 0:1], AF.Copy,
                                   bias=bias), "SS")
            bank_read(bi, "SS", last)
        return last

    ppbf_last_st = [0]

    def pp_to_row(pptile, colbase, dstrow):
        P.wait("vector", "ST", ppbf_last_st[0])
        svc = op("vector", lambda e: e.tensor_copy(ppv_bf[:, :],
                                                   pptile[:, colbase:colbase + 8]), "SV")
        P.wait("tensor", "SV", svc)
        last = None
        for t in range(8):
            bi = 4 + (t % 2)
            use_bank(bi)
            stv = mm(pb[bi][0:1, 0:128], ppv_bf[:, t:t + 1], ident[:, 0:128],
                     True, True, mark=True)
            ppbf_last_st[0] = stv
            P.wait("scalar", "ST", stv)
            last = op("scalar", lambda e, t=t, bi=bi, dstrow=dstrow:
                      e.activation(dstrow[0:1, t * 128:(t + 1) * 128],
                                   pb[bi][0:1, 0:128], AF.Copy), "SS")
            bank_read(bi, "SS", last)
        return last

    def outer_rep(rowap, dst, bias=0.0):
        stv = None
        for qh in range(2):
            bi = 4 + qh
            use_bank(bi)
            stv = mm(pb[bi][:, 0:512], ones1r[0:1, 0:128],
                     rowap[0:1, qh * 512:(qh + 1) * 512], True, True, mark=True)
        P.wait("scalar", "ST", stv)
        last = None
        for qh in range(2):
            bi = 4 + qh
            last = op("scalar", lambda e, qh=qh, bi=bi, dst=dst:
                      e.activation(dst[:, qh * 512:(qh + 1) * 512],
                                   pb[bi][:, 0:512], AF.Copy, bias=bias), "SS")
            bank_read(bi, "SS", last)
        return last

    v_vid = dma("sync", vid_s[0:1, 0:1], vid[0:1, 0:1])
    v_id = dma("sync", ident[:, :], identd[:, :])
    v_c = gdma(rowmm2[0:1, :], cvec[0:1, :], "SG0")
    v_ws = gdma(ppv[0:1, 48:49], wscale[0:1, 0:1], "SG0")
    op("vector", lambda e: e.memset(ones128[:, :], 1.0))
    op("vector", lambda e: e.memset(ones2d[:, :], 1.0))
    op("vector", lambda e: e.memset(ones1r[0:1, :], 1.0))
    sv_ones = op("vector", lambda e: e.memset(negm[:, :], -MSHIFT), "SV")

    P.wait("scalar", "SG0", v_c)
    op("scalar", lambda e: e.activation(rowfa[0:1, :], rowmm2[0:1, :], AF.Exp,
                                        scale=-1.0))
    ss0 = op("scalar", lambda e: e.activation(rowfa[0:1, :], rowfa[0:1, :],
                                              AF.Copy, bias=1.0), "SS")
    P.wait("vector", "SS", ss0)
    op("vector", lambda e: e.reciprocal(rowfa[0:1, :], rowfa[0:1, :]))
    vnop()
    sv_s = op("vector", lambda e: e.tensor_mul(rowmm[0:1, :], rowmm2[0:1, :],
                                               rowfa[0:1, :]), "SV")
    P.wait("tensor", "SV", sv_s)
    P.wait("tensor", "SV", sv_ones)
    ss_spp = row_to_pp(rowmm[0:1, :], s_pp, 0)
    P.wait("tensor", "SS", ss_spp)
    P.wait("vector", "SG0", v_ws)
    P.wait("vector", "ST", P.ctr["ST"])
    op("vector", lambda e: e.tensor_copy(rowmm[0:1, 0:1], ppv[0:1, 48:49]))
    vnop()
    op("vector", lambda e: e.tensor_sub(rowfa[0:1, 0:1], ppv[0:1, 48:49],
                                        rowmm[0:1, 0:1]))
    vnop()
    sv_wlo = op("vector", lambda e: e.tensor_copy(rowmm2[0:1, 0:1],
                                                  rowfa[0:1, 0:1]), "SV")
    P.wait("tensor", "SV", sv_wlo)
    use_bank(4)
    mm(pb[4][:, 0:1], ones1r[0:1, 0:128], rowmm[0:1, 0:1], True, False)
    stv = mm(pb[4][:, 0:1], ones1r[0:1, 0:128], rowmm2[0:1, 0:1], False, True,
             mark=True)
    P.wait("scalar", "ST", stv)
    ss_ws = op("scalar", lambda e: e.activation(ppv[:, 48:49], pb[4][:, 0:1],
                                                AF.Copy), "SS")
    bank_read(4, "SS", ss_ws)

    vload_q = {0: gdma(wstage[:].rearrange("p (s c) -> p s c", c=1024),
                       wqkv[0:D, :].rearrange("(s p) c -> p s c", p=128),
                       "SG2"),
               1: gdma(wstage2[:].rearrange("p (s c) -> p s c", c=1024),
                       wqkv[D:2 * D, :].rearrange("(s p) c -> p s c", p=128),
                       "SG3")}

    mod_tgt = {0: (ppv, 0, 0.0), 1: (ppv, 8, 1.0), 2: (ppv2, 0, 0.0),
               3: (ppv2, 8, 0.0), 4: (ppv2, 16, 1.0), 5: (ppv2, 24, 0.0)}
    mod_st = {}
    for g in range(12):
        buf = wmb0 if g % 2 == 0 else wmb1
        sem = "SG0" if g % 2 == 0 else "SG1"
        if g >= 2:
            P.wait("gpsimd", "ST", mod_st[g - 2])
        vload = gdma(buf[:].rearrange("p (cc j) -> p cc j", j=512),
                     wmod[:, g * 512:(g + 1) * 512].rearrange(
                         "(cc p) j -> p cc j", p=128), sem)
        P.wait("tensor", sem, vload)
        bi = 4 + g % 2
        use_bank(bi)
        stv = None
        for cc in range(CC):
            stv = mm(pb[bi][0:1, 0:512], s_pp[:, cc:cc + 1],
                     buf[:, cc * 512:(cc + 1) * 512],
                     cc == 0, cc == CC - 1, mark=(cc == CC - 1))
        mod_st[g] = stv
        P.wait("scalar", "ST", stv)
        prm, half = g // 2, g % 2
        ss = op("scalar", lambda e, bi=bi, half=half:
                e.activation(rowmm[0:1, half * 512:(half + 1) * 512],
                             pb[bi][0:1, 0:512], AF.Copy), "SS")
        bank_read(bi, "SS", ss)
        if half == 1:
            if prm in (2, 5):
                P.wait("vector", "SS", ss)
                op("vector", lambda e: e.tensor_scalar(rowfa[0:1, :],
                                                       rowmm[0:1, :], 0.0, None, OP.is_ge))
                vnop()
                op("vector", lambda e: e.tensor_scalar(rowfa[0:1, :],
                                                       rowfa[0:1, :], 2.0, -1.0, OP.mult, OP.add))
                ssa = op("scalar", lambda e: e.activation(rowmm2[0:1, :],
                                                          rowmm[0:1, :], AF.Abs), "SS")
                P.wait("vector", "SS", ssa)
                op("vector", lambda e: e.tensor_scalar(rowmm2[0:1, :],
                                                       rowmm2[0:1, :], 0.25, None, OP.max))
                vnop()
                svc = op("vector", lambda e: e.tensor_mul(rowmm[0:1, :],
                                                          rowfa[0:1, :], rowmm2[0:1, :]), "SV")
                P.wait("tensor", "SV", svc)
            P.wait("tensor", "SS", ss)
            tile, colbase, bias = mod_tgt[prm]
            row_to_pp(rowmm[0:1, :], tile, colbase, bias=bias)

    P.wait("vector", "SS", P.ctr["SS"])
    sv_caq = op("vector", lambda e: e.tensor_scalar(ppv[:, 16:24], ppv[:, 8:16],
                                                    1.0 / 32.0, None, OP.mult), "SV")
    sv_sabf = op("vector", lambda e: e.tensor_copy(ppv_bf[:, :], ppv[:, 0:8]),
                 "SV")

    P.wait("tensor", "SD", v_id)
    wq_tr = {}
    for grp in range(3):
        if grp == 2:
            P.wait("gpsimd", "ST", wq_tr[0]["st"])
            vload_q[2] = gdma(wstage[:].rearrange("p (s c) -> p s c", c=1024),
                              wqkv[2 * D:3 * D, :].rearrange(
                                  "(s p) c -> p s c", p=128), "SG2")
        stg = wstage2 if grp == 1 else wstage
        sem = "SG3" if grp == 1 else "SG2"
        P.wait("tensor", sem, vload_q[grp])
        last = {"st": 0, "ss": 0, "sv": 0}
        for s in range(8):
            if grp == 0:
                dst, dstride, doff = wxT, 1024, s * 128
            else:
                dst, dstride, doff = wkvT, 2048, (grp - 1) * 1024 + s * 128
            ev = ("scalar", "vector")[s % 2]
            stv, esem, ev_v = pe_tr8(stg, s * 1024, dst, dstride, doff,
                                     (0, 1, 2, 3)[s % 4], ev)
            last["st"] = stv
            last[esem.lower()] = max(last[esem.lower()], ev_v)
        wq_tr[grp] = last

    P.wait("tensor", "SS", max(wq_tr[g]["ss"] for g in range(3)))
    P.wait("tensor", "SV", max(wq_tr[g]["sv"] for g in range(3)))
    P.wait("tensor", "SV", sv_sabf)

    def bias_matvec(wsel, dstcol, scale):
        for jt in range(8):
            bi = 4 + (jt % 2)
            use_bank(bi)
            stv = None
            for cc in range(CC):
                if wsel == "q":
                    lh = wxT[:, cc * 1024 + jt * 128: cc * 1024 + jt * 128 + 128]
                else:
                    lh = wkvT[:, cc * 2048 + 1024 + jt * 128: cc * 2048 + 1024 + jt * 128 + 128]
                stv = mm(pb[bi][0:128, 0:1], lh, ppv_bf[:, cc:cc + 1],
                         cc == 0, cc == CC - 1, mark=(cc == CC - 1))
            ppbf_last_st[0] = stv
            P.wait("scalar", "ST", stv)
            ssx = op("scalar", lambda e, jt=jt, bi=bi:
                     e.activation(ppv[:, dstcol + jt:dstcol + jt + 1],
                                  pb[bi][0:128, 0:1], AF.Copy, scale=scale), "SS")
            bank_read(bi, "SS", ssx)

    bias_matvec("q", 32, 1.0 / 32.0)
    bias_matvec("v", 24, 1.0)

    P.wait("scalar", "SV", sv_caq)
    ss_wscale = None
    for cc in range(CC):
        ss_wscale = op("scalar", lambda e, cc=cc:
                       e.mul(wxT[:, cc * 1024:(cc + 1) * 1024],
                             wxT[:, cc * 1024:(cc + 1) * 1024],
                             ppv[:, 16 + cc:17 + cc]), "SS")
        ss_wscale = op("scalar", lambda e, cc=cc:
                       e.mul(wkvT[:, cc * 2048:(cc + 1) * 2048],
                             wkvT[:, cc * 2048:(cc + 1) * 2048],
                             ppv[:, 8 + cc:9 + cc]), "SS")

    P.wait("sync", "ST", mod_st[11])
    P.wait("scalar", "ST", mod_st[11])
    tr_hist = {}
    qgen_tr = {"ss": 0, "sv": 0}
    qg_lnv = {}
    for blk in range(2):
        dstbuf = lnfm if blk == 0 else kfm
        for qtr in range(4):
            v_x = dma("sync",
                      xin[:].rearrange("p (s c) -> p s c", c=1024),
                      xq[blk * TB + qtr * 256: blk * TB + (qtr + 1) * 256, :]
                      .rearrange("(s p) c -> p s c", p=128))
            P.wait("vector", "SD", v_x)
            P.wait("scalar", "SD", v_x)
            for tt in range(2):
                t_idx = qtr * 2 + tt
                g_idx = blk * 8 + t_idx
                if g_idx >= 2:
                    pv = tr_hist.get(g_idx - 2)
                    if pv:
                        P.wait("scalar", "ST", pv["st"])
                    if g_idx - 2 in qg_lnv:
                        P.wait("vector", *qg_lnv[g_idx - 2])
                lnv = ln_tile(xin[:, tt * 1024:(tt + 1) * 1024], t_idx, sqv)
                qg_lnv[g_idx] = lnv
                P.wait("tensor", *lnv)
                vals = pe_transpose_tile(t_idx, dstbuf, (6, 7, 2, 3)[t_idx % 4],
                                         ("scalar", "vector")[t_idx % 2])
                tr_hist[g_idx] = vals
                qgen_tr["ss"] = max(qgen_tr["ss"], vals["ss"])
                qgen_tr["sv"] = max(qgen_tr["sv"], vals["sv"])
            P.wait("sync", "SS", P.ctr["SS"])
            P.wait("sync", "SV", P.ctr["SV"])
    P.wait("tensor", "SS", qgen_tr["ss"])
    P.wait("tensor", "SV", qgen_tr["sv"])
    P.wait("tensor", "SS", ss_wscale)
    qb4 = [6, 7, 2, 3]
    for ct in range(8):
        for j in range(4):
            use_bank(qb4[j])
        stv = None
        for cc in range(CC):
            for j, (blk, qh) in enumerate([(0, 0), (0, 1), (1, 0), (1, 1)]):
                src = lnfm if blk == 0 else kfm
                stv = mm(pb[qb4[j]][:, 0:512],
                         wxT[:, cc * 1024 + ct * 128: cc * 1024 + ct * 128 + 128],
                         src[:, cc * 1024 + qh * 512: cc * 1024 + qh * 512 + 512],
                         cc == 0, cc == CC - 1,
                         mark=(cc == CC - 1 and j == 3))
        P.wait("scalar", "ST", stv)
        for j, (blk, qh) in enumerate([(0, 0), (0, 1), (1, 0), (1, 1)]):
            dstq = QA if blk == 0 else QB
            ssq = op("scalar", lambda e, ct=ct, j=j, dstq=dstq, qh=qh:
                     e.add(dstq[:, ct * 1024 + qh * 512: ct * 1024 + qh * 512 + 512],
                           pb[qb4[j]][:, 0:512], ppv[:, 32 + ct:33 + ct]), "SS")
            bank_read(qb4[j], "SS", ssq)

    P.wait("sync", "ST", P.ctr["ST"])
    v_strip = dma("sync", strip[:, :], stripd[:, :])
    P.wait("vector", "SD", v_strip)

    snap_base = P.snapshot()
    var_final = []
    for VAR in range(4):
        P.restore(snap_base)
        P.cur = P.bodies[VAR]
        iA, iB = VAR, 7 - VAR
        E = 8 - VAR
        decode_jobs = [(nr, h) for nr in range(8) for h in range(4)]
        dec_i = 0
        dec_store = 0
        dec_u8 = 0
        pv_done_st = P.ctr["ST"]
        exp_done_ss = P.ctr["SS"]
        kv_ss_hist = {}

        def decode_halftile(nr, h):
            nonlocal dec_store, dec_u8
            t0, t1, t2, t3 = dect
            P.wait("sync", "SV", dec_u8)
            v = dma("sync", u8in[:, :],
                    w8u[nr * 128:(nr + 1) * 128, h * 256:(h + 1) * 256])
            P.wait("vector", "SD", v)
            P.wait("vector", "SG1", dec_store)
            op("vector", lambda e: e.tensor_copy(t0[:, :], u8in[:, :]))
            vnop()
            op("vector", lambda e: e.tensor_scalar(
                t2[:, :], t0[:, :], 128.0, None, OP.is_ge))
            vnop()
            op("vector", lambda e: e.scalar_tensor_tensor(
                t0[:, :], t2[:, :], -128.0, t0[:, :], OP.mult, OP.add))
            vnop()
            op("vector", lambda e: e.tensor_scalar(
                msku[:, 0:256], t0[:, :], 8.0, None, OP.is_ge))
            op("vector", lambda e: e.tensor_scalar(
                t2[:, :], t0[:, :], 64.0, None, OP.is_ge))
            vnop()
            op("vector", lambda e: e.scalar_tensor_tensor(
                t0[:, :], t2[:, :], -64.0, t0[:, :], OP.mult, OP.add))
            op("vector", lambda e: e.tensor_scalar(
                t1[:, :], t2[:, :], 255.0, 1.0, OP.mult, OP.add))
            vnop()
            op("vector", lambda e: e.tensor_scalar(
                t2[:, :], t0[:, :], 32.0, None, OP.is_ge))
            vnop()
            op("vector", lambda e: e.scalar_tensor_tensor(
                t0[:, :], t2[:, :], -32.0, t0[:, :], OP.mult, OP.add))
            op("vector", lambda e: e.tensor_scalar(
                t3[:, :], t2[:, :], 15.0, 1.0, OP.mult, OP.add))
            vnop()
            op("vector", lambda e: e.tensor_mul(t1[:, :], t1[:, :], t3[:, :]))
            op("vector", lambda e: e.tensor_scalar(
                t2[:, :], t0[:, :], 16.0, None, OP.is_ge))
            vnop()
            op("vector", lambda e: e.scalar_tensor_tensor(
                t0[:, :], t2[:, :], -16.0, t0[:, :], OP.mult, OP.add))
            op("vector", lambda e: e.tensor_scalar(
                t3[:, :], t2[:, :], 3.0, 1.0, OP.mult, OP.add))
            vnop()
            op("vector", lambda e: e.tensor_mul(t1[:, :], t1[:, :], t3[:, :]))
            op("vector", lambda e: e.tensor_scalar(
                t2[:, :], t0[:, :], 8.0, None, OP.is_ge))
            vnop()
            op("vector", lambda e: e.scalar_tensor_tensor(
                t0[:, :], t2[:, :], -8.0, t0[:, :], OP.mult, OP.add))
            op("vector", lambda e: e.tensor_scalar(
                t3[:, :], t2[:, :], 1.0, 1.0, OP.mult, OP.add))
            vnop()
            op("vector", lambda e: e.tensor_mul(t1[:, :], t1[:, :], t3[:, :]))
            op("vector", lambda e: e.tensor_scalar(
                t3[:, :], t0[:, :], 2.0 ** -10, 8.0 * 2.0 ** -10, OP.mult, OP.add))
            vnop()
            op("vector", lambda e: e.tensor_mul(t3[:, :], t3[:, :], t1[:, :]))
            op("vector", lambda e: e.tensor_scalar(
                t0[:, :], t0[:, :], 2.0 ** -9, None, OP.mult))
            vnop()
            op("vector", lambda e: e.select(t1[:, :], msku[:, 0:256],
                                            t3[:, :], t0[:, :]))
            op("vector", lambda e: e.tensor_scalar(
                t2[:, :], u8in[:, :], 128.0, None, OP.is_ge))
            vnop()
            dec_u8 = op("vector", lambda e: e.tensor_scalar(
                t2[:, :], t2[:, :], -2.0, 1.0, OP.mult, OP.add), "SV")
            vnop()
            sv = op("vector", lambda e: e.tensor_mul(t1[:, :], t1[:, :],
                                                     t2[:, :]), "SV")
            P.wait("gpsimd", "SV", sv)
            dec_store = gdma(w8dec[nr * 128:(nr + 1) * 128, h * 256:(h + 1) * 256],
                             t1[:, :], "SG1")

        E_eff = min(E, KTRUNC) if KTRUNC else E
        ln_sv = {k2: ("SV", P.ctr["SV"]) for k2 in range(-8, 0)}

        def prep_load(k, t_idx):
            P.wait("gpsimd", *ln_sv[k * 8 + t_idx - 2])
            return gdma(xin[:, (t_idx % 2) * 1024:(t_idx % 2) * 1024 + 1024],
                        xs[k * TB + t_idx * 128: k * TB + (t_idx + 1) * 128, :],
                        "SG0")

        def prep_ln(k, t_idx, v_x):
            P.wait("vector", "SG0", v_x)
            g_idx = k * 8 + t_idx + 100
            pv = tr_hist.get(g_idx - 2)
            if pv:
                P.wait("scalar", "ST", pv["st"])
            P.wait("vector", *ln_sv[k * 8 + t_idx - 2])
            lnv = ln_tile(xin[:, (t_idx % 2) * 1024:(t_idx % 2) * 1024 + 1024],
                          t_idx, sqx)
            ln_sv[k * 8 + t_idx] = lnv

        def prep_tr(k, t_idx, tr_vals):
            g_idx = k * 8 + t_idx + 100
            P.wait("tensor", *ln_sv[k * 8 + t_idx])
            vals = pe_transpose_tile(t_idx, lnfm, (2, 3, 6, 7)[t_idx % 4],
                                     ("scalar", "vector")[t_idx % 2])
            tr_hist[g_idx] = vals
            tr_vals["ss"] = max(tr_vals["ss"], vals["ss"])
            tr_vals["sv"] = max(tr_vals["sv"], vals["sv"])

        tr_of = {}
        if E_eff > 0:
            tr_of[0] = {"ss": 0, "sv": 0}
            P.wait("vector", "ST", pv_done_st)
            for tt in range(8):
                v_x = prep_load(0, tt)
                prep_ln(0, tt, v_x)
                prep_tr(0, tt, tr_of[0])
        for k in range(E_eff):
            alive = ([("A", QA, OA, 0)] if k <= iA else []) + [("B", QB, OB, 1)]
            tr_vals = tr_of[k]
            P.wait("tensor", "SS", tr_vals["ss"])
            P.wait("tensor", "SV", tr_vals["sv"])
            gb4 = [2, 3, 6, 7]
            gbi = 0
            for ct in range(8):
                b0, b1 = gb4[gbi % 4], gb4[(gbi + 1) % 4]
                gbi += 2
                use_bank(b0)
                use_bank(b1)
                stv = None
                for cc in range(CC):
                    for kh, bi in ((0, b0), (1, b1)):
                        stv = mm(pb[bi][:, 0:512],
                                 wkvT[:, cc * 2048 + ct * 128: cc * 2048 + ct * 128 + 128],
                                 lnfm[:, cc * 1024 + kh * 512: cc * 1024 + kh * 512 + 512],
                                 cc == 0, cc == CC - 1,
                                 mark=(cc == CC - 1 and kh == 1))
                P.wait("scalar", "ST", stv)
                for kh, bi in ((0, b0), (1, b1)):
                    ssk = op("scalar", lambda e, ct=ct, kh=kh, bi=bi:
                             e.activation(kfm[:, ct * 1024 + kh * 512: ct * 1024 + kh * 512 + 512],
                                          pb[bi][:, 0:512], AF.Copy), "SS")
                    bank_read(bi, "SS", ssk)
            for tt in range(8):
                b0, b1 = gb4[gbi % 4], gb4[(gbi + 1) % 4]
                gbi += 2
                use_bank(b0)
                use_bank(b1)
                stv = None
                for cc in range(CC):
                    for dh, bi in ((0, b0), (1, b1)):
                        stv = mm(pb[bi][:, 0:512],
                                 lnfm[:, cc * 1024 + tt * 128: cc * 1024 + tt * 128 + 128],
                                 wkvT[:, cc * 2048 + 1024 + dh * 512: cc * 2048 + 1024 + dh * 512 + 512],
                                 cc == 0, cc == CC - 1,
                                 mark=(cc == CC - 1 and dh == 1))
                P.wait("scalar", "ST", stv)
                for dh, bi in ((0, b0), (1, b1)):
                    ssv = op("scalar", lambda e, tt=tt, dh=dh, bi=bi:
                             e.activation(vtm[:, tt * 1024 + dh * 512: tt * 1024 + dh * 512 + 512],
                                          pb[bi][:, 0:512], AF.Copy), "SS")
                    bank_read(bi, "SS", ssv)
            kv_evac_ss = P.ctr["SS"]
            if k + 1 < E_eff:
                tr_of[k + 1] = {"ss": 0, "sv": 0}
            if DEBUG and VAR == 0 and k == 0:
                P.wait("gpsimd", "SS", kv_evac_ss)
                odma(dbg["dbg_k"][:, :], kfm[:, :], sem="SG1")
                odma(dbg["dbg_v"][:, :], vtm[:, :], sem="SG1")
                odma(dbg["dbg_lnfm"][:, :], lnfm[:, :], sem="SG1")

            P.wait("tensor", "SS", kv_evac_ss)
            first_pair = True
            vx_next = 0
            for (nm, Qb, Ob, li) in alive:
                qblk = iA if nm == "A" else iB
                diag = (k == qblk)
                P.wait("scalar", "ST", pv_done_st)
                for kt in range(8):
                    sbk = (0, 1) if kt % 2 == 0 else (4, 5)
                    use_bank(sbk[0])
                    use_bank(sbk[1])
                    stv = None
                    skip0 = diag and kt >= 4
                    for cc in range(CC):
                        for qh in range(2):
                            if skip0 and qh == 0:
                                continue
                            stv = mm(pb[sbk[qh]][:, 0:512],
                                     kfm[:, cc * 1024 + kt * 128: cc * 1024 + kt * 128 + 128],
                                     Qb[:, cc * 1024 + qh * 512: cc * 1024 + qh * 512 + 512],
                                     cc == 0, cc == CC - 1,
                                     mark=(cc == CC - 1 and qh == 1))
                    P.wait("scalar", "ST", stv)
                    for qh in range(2):
                        if skip0 and qh == 0:
                            continue
                        exp_done_ss = op("scalar", lambda e, kt=kt, qh=qh, b=sbk[qh]:
                                         e.activation(pT[:, kt * 1024 + qh * 512: kt * 1024 + qh * 512 + 512],
                                                      pb[b][:, 0:512], AF.Exp,
                                                      bias=negm[:, 0:1]), "SS")
                        bank_read(sbk[qh], "SS", exp_done_ss)
                    if diag:
                        soff = 896 - kt * 128
                        h0 = 512 if skip0 else 0
                        P.wait("vector", "SS", exp_done_ss)
                        op("vector", lambda e, kt=kt, soff=soff, h0=h0:
                           e.tensor_mul(pT[:, kt * 1024 + h0:(kt + 1) * 1024],
                                        pT[:, kt * 1024 + h0:(kt + 1) * 1024],
                                        strip[:, soff + h0:soff + 1024]))
                    if first_pair and k + 1 < E_eff:
                        vx_next = prep_load(k + 1, kt)
                        prep_ln(k + 1, kt, vx_next)
                        if kt >= 2:
                            prep_tr(k + 1, kt - 2, tr_of[k + 1])
                if first_pair and k + 1 < E_eff:
                    prep_tr(k + 1, 6, tr_of[k + 1])
                    prep_tr(k + 1, 7, tr_of[k + 1])
                if diag:
                    svm = op("vector", lambda e: e.engine_nop(), "SV")
                    P.wait("tensor", "SV", svm)
                else:
                    P.wait("tensor", "SS", exp_done_ss)
                use_bank(4)
                use_bank(5)
                stv = None
                for kt in range(8):
                    for qh in range(2):
                        if diag and qh == 0 and kt >= 4:
                            continue
                        lastk = 3 if (diag and qh == 0) else 7
                        stv = mm(pb[4 + qh][0:128, 0:512], ones2d[:, :],
                                 pT[:, kt * 1024 + qh * 512: kt * 1024 + qh * 512 + 512],
                                 kt == 0, kt == lastk,
                                 mark=(kt == 7 and qh == 1))
                P.wait("vector", "ST", stv)
                svl = None
                for qh in range(2):
                    sl = lsb[0:1, li * 1024 + qh * 512: li * 1024 + qh * 512 + 512]
                    if k == 0:
                        svl = op("vector", lambda e, qh=qh, sl=sl:
                                 e.tensor_copy(sl, pb[4 + qh][0:1, 0:512]), "SV")
                    else:
                        svl = op("vector", lambda e, qh=qh, sl=sl:
                                 e.tensor_add(sl, sl, pb[4 + qh][0:1, 0:512]), "SV")
                bank_read(4, "SV", svl)
                bank_read(5, "SV", svl)
                pvb = [(2, 3), (6, 7)]
                svpv = None
                for dt in range(8):
                    bk = pvb[dt % 2]
                    use_bank(bk[0])
                    use_bank(bk[1])
                    stv = None
                    for kt in range(8):
                        for qh in range(2):
                            if diag and qh == 0 and kt >= 4:
                                continue
                            lastk = 3 if (diag and qh == 0) else 7
                            stv = mm(pb[bk[qh]][:, 0:512],
                                     vtm[:, kt * 1024 + dt * 128: kt * 1024 + dt * 128 + 128],
                                     pT[:, kt * 1024 + qh * 512: kt * 1024 + qh * 512 + 512],
                                     kt == 0, kt == lastk,
                                     mark=(kt == 7 and qh == 1))
                    P.wait("vector", "ST", stv)
                    for qh in range(2):
                        dst = Ob[:, dt * 1024 + qh * 512: dt * 1024 + qh * 512 + 512]
                        if k == 0:
                            svpv = op("vector", lambda e, dst=dst, b=bk[qh]:
                                      e.tensor_copy(dst, pb[b][:, 0:512]), "SV")
                        else:
                            svpv = op("vector", lambda e, dst=dst, b=bk[qh]:
                                      e.tensor_add(dst, dst, pb[b][:, 0:512]), "SV")
                        bank_read(bk[qh], "SV", svpv)
                P.wait("tensor", "SV", svpv)
                pv_done_st = P.bump("ST", 1)
                P.emit("tensor", lambda e: e.sem_inc(SEMS["ST"], 1))
                first_pair = False
            per = (len(decode_jobs) + E - 1) // E
            while dec_i < min(len(decode_jobs), (k + 1) * per):
                decode_halftile(*decode_jobs[dec_i])
                dec_i += 1
        while not KTRUNC and dec_i < len(decode_jobs):
            decode_halftile(*decode_jobs[dec_i])
            dec_i += 1
        if DEBUG and VAR == 0:
            P.wait("gpsimd", "SV", P.ctr["SV"])
            P.wait("gpsimd", "ST", P.ctr["ST"])
            odma(dbg["dbg_pt"][:, :], pT[:, :], sem="SG1")
            odma(dbg["dbg_q"][:, :], QA[:, :], sem="SG1")
            odma(dbg["dbg_o"][:, :], OA[:, :], sem="SG1")
            odma(dbg["dbg_l"][0:1, :], lsb[0:1, :], sem="SG1")
        var_final.append(P.snapshot())

    maxctr = {s: max(vf[0][s] for vf in var_final) for s in SEM_NAMES}
    for VAR in range(4):
        P.restore(var_final[VAR])
        P.cur = P.bodies[VAR]
        for s in SEM_NAMES:
            d = maxctr[s] - P.ctr[s]
            if d > 0:
                if s in ("SG0", "SG1"):
                    assert d % 16 == 0
                    for _ in range(d // 16):
                        gdma(lsb[0:1, 0:1], xs[0:1, 0:1], s)
                else:
                    P.emit(SEM_OWNER[s],
                           lambda e, sm=s, dd=d: e.sem_inc(SEMS[sm], dd))
    minwait = {e: {} for e in ENGS}
    for e in ENGS:
        keys = set()
        for vf in var_final:
            keys |= set(vf[1][e])
        for kk in keys:
            minwait[e][kk] = min(vf[1][e].get(kk, 0) for vf in var_final)
    bf_merged = {}
    for vf in var_final:
        for bi, (sem, val) in vf[2].items():
            cur = bf_merged.get(bi)
            if cur is None or (cur[0] == sem and val > cur[1]):
                bf_merged[bi] = (sem, val)
            elif cur[0] != sem:
                bf_merged[bi] = (sem, maxctr[sem])
    P.ctr = dict(maxctr)
    P.waited = minwait
    P.bank_free = bf_merged
    P.cur = P.post

    P.wait("sync", "ST", P.ctr["ST"])
    P.wait("sync", "SV", P.ctr["SV"])
    P.wait("sync", "SS", P.ctr["SS"])
    P.wait("scalar", "ST", P.ctr["ST"])
    P.wait("scalar", "SV", P.ctr["SV"])
    P.wait("gpsimd", "SG1", P.ctr["SG1"])
    P.wait("gpsimd", "ST", P.ctr["ST"])
    P.wait("gpsimd", "SV", P.ctr["SV"])
    P.wait("gpsimd", "SS", P.ctr["SS"])

    if not KTRUNC:
        sv_rl = op("vector", lambda e: e.reciprocal(lsb[0:1, :], lsb[0:1, :]), "SV")

        v_wo = gdma(wstage[:].rearrange("p (s c) -> p s c", c=1024),
                    wout[:, :].rearrange("(s p) c -> p s c", p=128), "SG0")
        P.wait("tensor", "SG0", v_wo)
        wo_tr = {"st": 0, "ss": 0, "sv": 0}
        for s in range(8):
            stv, esem, ev_v = pe_tr8(wstage, s * 1024, wxT, 1024, s * 128,
                                     (0, 1, 2, 3)[s % 4],
                                     ("scalar", "vector")[s % 2])
            wo_tr["st"] = stv
            wo_tr[esem.lower()] = max(wo_tr[esem.lower()], ev_v)
        ss_ga = pp_to_row(ppv2, 0, rowmm)
        P.wait("tensor", "SS", ss_ga)

        ss_garep = outer_rep(rowmm[0:1, :], foldrep)
        P.wait("vector", "SS", wo_tr["ss"])
        P.wait("vector", "SV", wo_tr["sv"])
        P.wait("vector", "SS", ss_garep)
        sv_fold = None
        for cc in range(CC):
            sv_fold = op("vector", lambda e, cc=cc:
                         e.tensor_mul(wxT[:, cc * 1024:(cc + 1) * 1024],
                                      wxT[:, cc * 1024:(cc + 1) * 1024],
                                      foldrep[:, :]), "SV")
        sv_bv = op("vector", lambda e: e.tensor_copy(bv_bf[:, :], ppv[:, 24:32]), "SV")
        P.wait("tensor", "SV", sv_fold)
        P.wait("tensor", "SV", sv_bv)
        use_bank(4)
        use_bank(5)
        stv = None
        for qh in range(2):
            for cc in range(CC):
                stv = mm(pb[4 + qh][0:1, 0:512], bv_bf[:, cc:cc + 1],
                         wxT[:, cc * 1024 + qh * 512: cc * 1024 + qh * 512 + 512],
                         cc == 0, cc == CC - 1,
                         mark=(cc == CC - 1 and qh == 1))
        P.wait("scalar", "ST", stv)
        ss_bvwo = None
        for qh in range(2):
            ss_bvwo = op("scalar", lambda e, qh=qh:
                         e.activation(rowfa[0:1, qh * 512:(qh + 1) * 512],
                                      pb[4 + qh][0:1, 0:512], AF.Copy), "SS")
            bank_read(4 + qh, "SS", ss_bvwo)
        v_ob = gdma(rowmm2[0:1, :], outbias[0:1, :], "SG0")
        P.wait("vector", "SS", ss_bvwo)
        P.wait("vector", "SG0", v_ob)

        op("vector", lambda e: e.tensor_mul(rowmm2[0:1, :], rowmm[0:1, :],
                                            rowmm2[0:1, :]))
        op("vector", lambda e: e.tensor_add(rowfa[0:1, :], rowfa[0:1, :],
                                            rowmm2[0:1, :]))
        sv_gab = op("vector", lambda e: e.tensor_copy(rowmm[0:1, :],
                                                      rowfa[0:1, :]), "SV")
        P.wait("tensor", "SV", sv_gab)
        ss_gabrep = outer_rep(rowmm[0:1, :], gab_rep)

        ss_r = pp_to_row(ppv2, 16, rowmm)
        P.wait("tensor", "SS", ss_r)
        ss_cb1 = outer_rep(rowmm[0:1, :], cb1_rep)
        ss_r = pp_to_row(ppv2, 8, rowmm)
        P.wait("tensor", "SS", ss_r)
        ss_sb = outer_rep(rowmm[0:1, :], sb_rep)
        ss_r = pp_to_row(ppv2, 24, rowmm)
        P.wait("gpsimd", "SV", sv_gab)
        v_fb = gdma(rowmm2[0:1, :], fbias[0:1, :], "SG0")
        P.wait("vector", "SS", ss_r)
        P.wait("vector", "SG0", v_fb)
        sv_gbb = op("vector", lambda e: e.tensor_mul(rowmm2[0:1, :], rowmm[0:1, :],
                                                     rowmm2[0:1, :]), "SV")
        P.wait("tensor", "SV", sv_gbb)
        ss_gbbrep = outer_rep(rowmm2[0:1, :], gbb_rep)

        P.wait("gpsimd", "ST", wo_tr["st"])
        v_w8 = gdma(wstage[:].rearrange("p (s c) -> p s c", c=1024),
                    w8dec[:, :].rearrange("(s p) c -> p s c", p=128), "SG0")
        P.wait("tensor", "SG0", v_w8)
        w8_tr = {"st": 0, "ss": 0, "sv": 0}
        for s in range(8):
            stv, esem, ev_v = pe_tr8(wstage, s * 1024, w8T, 1024, s * 128,
                                     (0, 1, 2, 3)[s % 4],
                                     ("scalar", "vector")[s % 2])
            w8_tr["st"] = stv
            w8_tr[esem.lower()] = max(w8_tr[esem.lower()], ev_v)
        ss_gbrep = outer_rep(rowmm[0:1, :], foldrep)
        P.wait("vector", "SS", w8_tr["ss"])
        P.wait("vector", "SV", w8_tr["sv"])
        P.wait("vector", "SS", ss_gbrep)
        sv_w8fold = None
        for cc in range(CC):
            sv_w8fold = op("vector", lambda e, cc=cc:
                           e.tensor_mul(w8T[:, cc * 1024:(cc + 1) * 1024],
                                        w8T[:, cc * 1024:(cc + 1) * 1024],
                                        foldrep[:, :]), "SV")
        if DEBUG:
            P.wait("gpsimd", "SV", sv_w8fold)
            odma(dbg["dbg_w8t"][:, :], w8T[:, :], sem="SG1")

        out_dma = {}
        x2_sp = {}
        trq = {}
        ffn_stv = {}
        xin_sv = {}
        qs_ss = {}
        q_lnv = {}
        rl_ss = [0]

        def rl_prep(li, Ob):
            P.wait("vector", "SV", sv_rl)
            P.wait("vector", "ST", P.ctr["ST"])
            lslice = lsb[0:1, li * 1024:(li + 1) * 1024]
            op("vector", lambda e, lslice=lslice:
               e.tensor_copy(rowmm[0:1, :], lslice))
            op("vector", lambda e, lslice=lslice:
               e.tensor_sub(rowfa[0:1, :], lslice, rowmm[0:1, :]))
            sv_lo = op("vector", lambda e: e.tensor_copy(rowmm2[0:1, :],
                                                         rowfa[0:1, :]), "SV")
            P.wait("tensor", "SV", sv_lo)
            use_bank(4)
            use_bank(5)
            stv = None
            for qh in range(2):
                mm(pb[4 + qh][:, 0:512], ones1r[0:1, 0:128],
                   rowmm[0:1, qh * 512:(qh + 1) * 512], True, False)
                stv = mm(pb[4 + qh][:, 0:512], ones1r[0:1, 0:128],
                         rowmm2[0:1, qh * 512:(qh + 1) * 512], False, True,
                         mark=(qh == 1))
            P.wait("scalar", "ST", stv)
            ss_rep = None
            for qh in range(2):
                ss_rep = op("scalar", lambda e, qh=qh:
                            e.activation(rlrep[:, qh * 512:(qh + 1) * 512],
                                         pb[4 + qh][:, 0:512], AF.Copy), "SS")
                bank_read(4 + qh, "SS", ss_rep)
            rl_ss[0] = ss_rep
            P.wait("vector", "SS", ss_rep)
            sv_on = None
            for dt in range(8):
                sv_on = op("vector", lambda e, dt=dt, Ob=Ob:
                           e.tensor_mul(Ob[:, dt * 1024:(dt + 1) * 1024],
                                        Ob[:, dt * 1024:(dt + 1) * 1024],
                                        rlrep[:, :]), "SV")
            return sv_on

        gb4 = [2, 3, 6, 7]
        v_x_cur = {}

        def outproj_tile(li, Ob, qt, sv_on):
            x2b = x2buf if li == 0 else x2bufB
            if qt == 0:
                P.wait("tensor", "SV", sv_on)
            if qt % 2 == 0:
                qtr = qt // 2
                pv = max(xin_sv.get((li, qt - 2), 0), xin_sv.get((li, qt - 1), 0),
                         xin_sv.get((li - 1, 6), 0), xin_sv.get((li - 1, 7), 0))
                P.wait("sync", "SV", pv)
                v_x_cur[li] = dma("sync",
                                  xin[:].rearrange("p (s c) -> p s c", c=1024),
                                  xq[li * TB + qtr * 256: li * TB + (qtr + 1) * 256, :]
                                  .rearrange("(s p) c -> p s c", p=128))
            qt2 = qt % 2
            b0, b1 = ((0, 1), (4, 5))[qt % 2]
            use_bank(b0)
            use_bank(b1)
            stv = None
            for dc in range(CC):
                for ch, bi in ((0, b0), (1, b1)):
                    stv = mm(pb[bi][:, 0:512],
                             Ob[:, dc * 1024 + qt * 128: dc * 1024 + qt * 128 + 128],
                             wxT[:, dc * 1024 + ch * 512: dc * 1024 + ch * 512 + 512],
                             dc == 0, dc == CC - 1,
                             mark=(dc == CC - 1 and ch == 1))
            P.wait("vector", "ST", stv)
            P.wait("vector", "SD", v_x_cur[li])
            svv = 0
            for ch, bi in ((0, b0), (1, b1)):
                dst = x2b[:, qt * 1024 + ch * 512: qt * 1024 + ch * 512 + 512]
                svv = op("vector", lambda e, dst=dst, bi=bi, qt2=qt2, ch=ch:
                         e.scalar_tensor_tensor(
                             dst, pb[bi][:, 0:512], 1.0,
                             xin[:, qt2 * 1024 + ch * 512: qt2 * 1024 + ch * 512 + 512],
                             OP.mult, OP.add), "SV")
                bank_read(bi, "SV", svv)
            xin_sv[(li, qt)] = svv
            P.wait("gpsimd", "SV", svv)
            P.wait("gpsimd", "SS", ss_gabrep)
            spv = 0
            for ch in range(2):
                dst = x2b[:, qt * 1024 + ch * 512: qt * 1024 + ch * 512 + 512]
                spv = op("gpsimd", lambda e, dst=dst, ch=ch:
                         e.tensor_add(dst, dst,
                                      gab_rep[:, ch * 512:(ch + 1) * 512]), "SP")
            x2_sp[(li, qt)] = spv

        def qa(li, qt):
            j = li * 8 + qt
            x2b = x2buf if li == 0 else x2bufB
            x2t = x2b[:, qt * 1024:(qt + 1) * 1024]
            P.wait("vector", "SP", x2_sp[(li, qt)])
            pv = trq.get(j - 2)
            if pv:
                P.wait("scalar", "SD", pv["sync"])
                P.wait("scalar", "SDS", pv["scalar"])
            if j - 2 in q_lnv:
                P.wait("vector", *q_lnv[j - 2])
            q_lnv[j] = ln_tile(x2t, qt, sqe)

        def qb(li, qt):
            j = li * 8 + qt
            qs0_ = qs0 if j % 2 == 0 else qs0b
            h2t = lntm[:, (qt % 2) * 1024:(qt % 2) * 1024 + 1024]
            P.wait("vector", *q_lnv[j])
            op("vector", lambda e, h2t=h2t: e.tensor_mul(h2t, h2t, cb1_rep))
            op("vector", lambda e, h2t=h2t: e.tensor_add(h2t, h2t, sb_rep))
            c0 = (qt % 2) * 8
            am = stats[:, c0 + 6:c0 + 7]
            rq = stats[:, c0 + 7:c0 + 8]
            op("vector", lambda e, h2t=h2t, am=am:
               e.reduce_max(am, h2t, AX.X, apply_absolute_value=True))
            vnop()
            op("vector", lambda e, am=am:
               e.tensor_scalar(am, am, 1e-12, None, OP.max))
            vnop()
            op("vector", lambda e, am=am, rq=rq: e.reciprocal(rq, am))
            vnop()
            op("vector", lambda e, rq=rq:
               e.tensor_scalar(rq, rq, 448.0, None, OP.mult))
            vnop()
            sv_dq = op("vector", lambda e, am=am, qt=qt:
                       e.tensor_scalar(ppv[:, 40 + qt:41 + qt], am,
                                       ppv[:, 48:49], 1.0 / 448.0,
                                       OP.mult, OP.mult), "SV")
            P.wait("scalar", "SV", sv_dq)
            ss_v = op("scalar", lambda e, h2t=h2t, rq=rq, qs0_=qs0_:
                      e.mul(qs0_[:, :], h2t, rq), "SS")
            qs_ss[(li, qt)] = ss_v

        def q3(li, qt):
            j = li * 8 + qt
            qs0_, qs1_, qs2_ = (qs0, qs1, qs2) if j % 2 == 0 else (qs0b, qs1b, qs2b)
            if li == 1:
                pa = trq[qt]
                P.wait("vector", "SD", pa["sync"])
                P.wait("vector", "SDS", pa["scalar"])
            P.wait("vector", "SS", qs_ss[(li, qt)])
            op("vector", lambda e, qs0_=qs0_, qs1_=qs1_:
               e.tensor_scalar(qs1_[:, :], qs0_[:, :], C1, None, OP.mult))
            op("vector", lambda e, qs0_=qs0_, qs1_=qs1_, qs2_=qs2_:
               e.tensor_sub(qs2_[:, :], qs1_[:, :], qs0_[:, :]))
            op("vector", lambda e, qs1_=qs1_, qs2_=qs2_:
               e.tensor_sub(qs1_[:, :], qs1_[:, :], qs2_[:, :]))
            sv_q8 = 0
            for hh in range(2):
                sl = slice(hh * 512, (hh + 1) * 512)
                op("vector", lambda e, sl=sl, qs0_=qs0_, qs2_=qs2_: e.tensor_mul(
                    qs2_[:, sl], qs0_[:, sl], qs0_[:, sl]))
                op("vector", lambda e, sl=sl, qs2_=qs2_: e.tensor_scalar(
                    msku[:, :], qs2_[:, sl], float(2.0 ** -12), None, OP.is_ge))
                op("vector", lambda e, sl=sl, qs0_=qs0_, qs2_=qs2_: e.tensor_scalar(
                    qs2_[:, sl], qs0_[:, sl], MAGIC, -MAGIC, OP.add, OP.add))
                sv_q8 = op("vector", lambda e, qt=qt, sl=sl, hh=hh, qs1_=qs1_, qs2_=qs2_:
                           e.select(q8tm[:, qt * 1024 + hh * 512: qt * 1024 + (hh + 1) * 512],
                                    msku[:, :], qs1_[:, sl], qs2_[:, sl]), "SV")
            if DEBUG and li == 0 and qt == 7:
                P.wait("gpsimd", "SV", sv_q8)
                odma(dbg["dbg_q8"][:, :], q8tm[:, :], sem="SG1")
            P.wait("sync", "SV", sv_q8)
            if li == 1:
                P.wait("sync", "ST", ffn_stv[(0, qt)])
            vals = {"sync": 0, "scalar": 0}
            for cc in range(CC):
                vals["sync"] = dma("sync",
                                   q8fm[:, cc * 1024 + qt * 128: cc * 1024 + qt * 128 + 128],
                                   q8tm[:, qt * 1024 + cc * 128: qt * 1024 + cc * 128 + 128],
                                   transpose=True)
            trq[j] = vals

        def ffn_tile(li, qt):
            x2b = x2buf if li == 0 else x2bufB
            b0, b1 = gb4[(qt % 2) * 2], gb4[(qt % 2) * 2 + 1]
            use_bank(b0)
            use_bank(b1)
            P.wait("tensor", "SD", trq[li * 8 + qt]["sync"])
            P.wait("tensor", "SDS", trq[li * 8 + qt]["scalar"])
            if li == 0 and qt == 0:
                P.wait("tensor", "SV", sv_w8fold)
            stv = None
            for cc in range(CC):
                for nh, bi in ((0, b0), (1, b1)):
                    stv = mm(pb[bi][:, 0:512],
                             q8fm[:, cc * 1024 + qt * 128: cc * 1024 + qt * 128 + 128],
                             w8T[:, cc * 1024 + nh * 512: cc * 1024 + nh * 512 + 512],
                             cc == 0, cc == CC - 1,
                             mark=(cc == CC - 1 and nh == 1))
            ffn_stv[(li, qt)] = stv
            ostg = ost0 if qt % 2 == 0 else ost1
            prev = out_dma.get((li, qt - 2)) or out_dma.get((li - 1, qt + 6))
            if prev:
                P.wait("vector", prev[1], prev[0])
            P.wait("vector", "ST", stv)
            stt = 0
            for nh, bi in ((0, b0), (1, b1)):
                stt = op("vector", lambda e, bi=bi, qt=qt, nh=nh, ostg=ostg:
                         e.scalar_tensor_tensor(
                             ostg[:, nh * 512:(nh + 1) * 512], pb[bi][:, 0:512],
                             ppv[:, 40 + qt:41 + qt],
                             x2b[:, qt * 1024 + nh * 512: qt * 1024 + nh * 512 + 512],
                             OP.mult, OP.add), "SV")
                bank_read(bi, "SV", stt)
            P.wait("gpsimd", "SV", stt)
            P.wait("gpsimd", "SS", ss_gbbrep)
            for nh in range(2):
                op("gpsimd", lambda e, nh=nh, ostg=ostg:
                   e.tensor_add(ostg[:, nh * 512:(nh + 1) * 512],
                                ostg[:, nh * 512:(nh + 1) * 512],
                                gbb_rep[:, nh * 512:(nh + 1) * 512]), "SP")
            out_dma[(li, qt)] = odma(
                out[li * TB + qt * 128: li * TB + (qt + 1) * 128, :], ostg[:, :],
                sem="SOUT" if qt % 2 == 0 else "SOUT1")

    if not KTRUNC:
        sv_on_a = rl_prep(0, OA)
        for qt in range(8):
            outproj_tile(0, OA, qt, sv_on_a)
        sv_on_b = rl_prep(1, OB)
        for qt in range(8):
            qa(0, qt)
            if qt >= 1:
                qb(0, qt - 1)
            if qt >= 2:
                q3(0, qt - 2)
            outproj_tile(1, OB, qt, sv_on_b)
            if qt >= 4:
                ffn_tile(0, qt - 4)
        for qt in range(8):
            qa(1, qt)
            qb(1, qt - 1) if qt >= 1 else qb(0, 7)
            if qt == 0:
                q3(0, 6)
            elif qt == 1:
                q3(0, 7)
            else:
                q3(1, qt - 2)
            if 1 <= qt <= 4:
                ffn_tile(0, qt + 3)
            elif qt >= 5:
                ffn_tile(1, qt - 5)
        qb(1, 7)
        q3(1, 6)
        q3(1, 7)
        for m in (3, 4, 5, 6, 7):
            ffn_tile(1, m)
        P.wait("gpsimd", "SOUT", P.ctr["SOUT"])
        P.wait("gpsimd", "SOUT1", P.ctr["SOUT1"])
    else:
        P.wait("gpsimd", "ST", P.ctr["ST"])
        P.wait("gpsimd", "SV", P.ctr["SV"])
        P.wait("gpsimd", "SS", P.ctr["SS"])
        odma(dbg["dbg_lnfm"][:, :], lnfm[:, :], sem="SG1")
        odma(dbg["dbg_k"][:, :], kfm[:, :], sem="SG1")
        odma(dbg["dbg_v"][:, :], vtm[:, :], sem="SG1")
        odma(dbg["dbg_q"][:, :], QA[:, :], sem="SG1")
        odma(dbg["dbg_pt"][:, :], pT[:, :], sem="SG1")
        odma(dbg["dbg_l"][0:1, :], lsb[0:1, :], sem="SG1")
        odma(dbg["dbg_o"][:, :], OA[:, :], sem="SG1")
        odma(dbg["dbg_x2"][:, :], OB[:, :], sem="SG1")
        P.wait("gpsimd", "SG1", P.ctr["SG1"])

    with nc.Block() as blk:
        for eng in ENGS:
            getattr(blk, eng)(lambda e, _o=P.pre[eng]: [f(e) for f in _o])
    engs = [nc.tensor, nc.vector, nc.scalar, nc.sync, nc.gpsimd]
    idx = []
    for e in engs:
        e.wait_ge(SEMS["SD"], 16)
        idx.append(e.value_load(in_tensor=vid_s[0:1, 0:1]))
    for i in nc.Switch(engines=engs, index=idx, n=4):
        with nc.Block() as blk2:
            for eng in ENGS:
                getattr(blk2, eng)(lambda e, _o=P.bodies[i][eng]: [f(e) for f in _o])
    with nc.Block() as blk3:
        for eng in ENGS:
            getattr(blk3, eng)(lambda e, _o=P.post[eng]: [f(e) for f in _o])
    return nc


_NC_CACHE = {}


def _get_nc():
    if "nc" not in _NC_CACHE:
        _NC_CACHE["nc"] = build_nc()
    return _NC_CACHE["nc"]


def make_in_maps(L0_attn_x, L0_attn_qkv_w, L0_attn_out_w, L0_attn_out_b,
                 L0_ffn_weight, L0_ffn_weight_scale, L0_ffn_bias, L0_c, L0_wmod):
    x = np.ascontiguousarray(np.asarray(L0_attn_x, dtype=np.float32))
    wqkv_a = np.ascontiguousarray(np.asarray(L0_attn_qkv_w, dtype=np.float32))
    wout_a = np.ascontiguousarray(np.asarray(L0_attn_out_w, dtype=np.float32))
    outb = np.ascontiguousarray(np.asarray(L0_attn_out_b, dtype=np.float32)).reshape(1, D)
    w8u_a = np.ascontiguousarray(np.asarray(L0_ffn_weight).view(np.uint8))
    ws = np.asarray(L0_ffn_weight_scale, dtype=np.float32).reshape(1, 1)
    fb = np.ascontiguousarray(np.asarray(L0_ffn_bias, dtype=np.float32)).reshape(1, D)
    cc_ = np.ascontiguousarray(np.asarray(L0_c, dtype=np.float32))
    wm = np.ascontiguousarray(np.asarray(L0_wmod, dtype=np.float32))
    u = np.arange(1920)[None, :]
    r = np.arange(128)[:, None]
    stripv = np.ascontiguousarray(((u - 896) >= r).astype(ml_dtypes.bfloat16))
    in_maps = []
    for c in range(8):
        b, i = c // 4, c % 4
        xb = x[b]
        xqv = np.ascontiguousarray(np.concatenate(
            [xb[i * TB:(i + 1) * TB], xb[(7 - i) * TB:(8 - i) * TB]], axis=0))
        in_maps.append({
            "vid": np.array([[i]], dtype=np.int32),
            "identd": np.eye(128, dtype=np.float32).astype(ml_dtypes.bfloat16),
            "xs": xb, "xq": xqv, "wqkv": wqkv_a, "wout": wout_a,
            "w8u": w8u_a, "wscale": ws, "fbias": fb, "outbias": outb,
            "cvec": np.ascontiguousarray(cc_[b]).reshape(1, D),
            "wmod": wm, "stripd": stripv,
        })
    return in_maps


def kernel(L0_attn_x, L0_attn_qkv_w, L0_attn_out_w, L0_attn_out_b,
           L0_ffn_weight, L0_ffn_weight_scale, L0_ffn_bias, L0_c, L0_wmod):
    nc = _get_nc()
    in_maps = make_in_maps(L0_attn_x, L0_attn_qkv_w, L0_attn_out_w,
                           L0_attn_out_b, L0_ffn_weight, L0_ffn_weight_scale,
                           L0_ffn_bias, L0_c, L0_wmod)
    res = run_bass_kernel_spmd(nc, in_maps, core_ids=list(range(8)))
    outv = np.empty((2, 8 * TB, D), dtype=np.float32)
    for c in range(8):
        b, i = c // 4, c % 4
        r = res.results[c]["out"]
        outv[b, i * TB:(i + 1) * TB] = r[0:TB]
        outv[b, (7 - i) * TB:(8 - i) * TB] = r[TB:2 * TB]
    return outv


if __name__ == "__main__":
    nc = build_nc()
    n_instr = sum(len(b.instructions) for b in nc.m.functions[0].blocks) \
        if hasattr(nc, "m") else -1
    print("build ok")
